# version 15
# baseline (speedup 1.0000x reference)
import numpy as np
from contextlib import ExitStack

import concourse.bass as bass
import concourse.mybir as mybir
from concourse.bass_utils import run_bass_kernel_spmd

F32 = mybir.dt.float32
BF16 = mybir.dt.bfloat16
AF = mybir.ActivationFunctionType
ALU = mybir.AluOpType

D = 1024
T = 2048
H = 16
TW = 512
NW = T // TW
TB = 1024
NCORE = 8

ENGS = ("t", "v", "s", "g", "q")
DMASEMS = ("g0", "g1", "gi", "qi", "q0", "q1", "q2", "qs")
ALLSEMS = ENGS + DMASEMS


class Prog:
    def __init__(self):
        self.ops = {e: [] for e in ENGS}
        self.cnt = {e: 0 for e in ALLSEMS}
        self.lastw = {e: {f: 0 for f in ALLSEMS} for e in ENGS}

    def wait(self, eng, on, val):
        if val <= 0 or eng == on:
            return
        assert val <= self.cnt[on], (eng, on, val, self.cnt[on])
        if self.lastw[eng][on] >= val:
            return
        self.lastw[eng][on] = val
        self.ops[eng].append(("w", on, val))

    def wait_self_dma(self, eng, on, val):
        if val <= 0:
            return
        self.ops[eng].append(("w", on, val))

    def emit(self, eng, fn, inc=0, sem=None):
        tgt = sem if sem is not None else eng
        self.ops[eng].append(("o", fn, inc, tgt))
        if inc:
            self.cnt[tgt] += inc
        return self.cnt[tgt]

    def run(self, eng, h, sems):
        for item in self.ops[eng]:
            if item[0] == "w":
                h.wait_ge(sems[item[1]], item[2])
            else:
                ins = item[1](h)
                if item[2]:
                    ins.then_inc(sems[item[3]], item[2])


def build(debug=False):
    nc = bass.Bass("TRN2", target_bir_lowering=False)

    def din(name, shape, dtype=F32):
        return nc.dram_tensor(name, list(shape), dtype, kind="ExternalInput").ap()

    xT = din("xT", (D, T))
    wqk = din("wqk", (4, 128, 4096))
    wvr = din("wvr", (2, 128, 4096))
    vb = din("vb", (1, D))
    qkb = din("qkb", (128, 16))
    outw = din("outw", (2, 128, 4096))
    outb = din("outb", (128, 8))
    w12 = din("w12", (32, 128, 2048))
    s1c = din("s1c", (128, 32))
    s2c = din("s2c", (128, 32))
    h1p = din("h1p", (32, 128, 4096))
    s1hc = din("s1hc", (128, 32))
    h2p = din("h2p", (32, 128, 4096))
    s2hc = din("s2hc", (128, 32))
    w3p = din("w3p", (8, 128, 4096))
    s3c = din("s3c", (128, 8))
    gam = din("gam", (128, H))
    ident = din("ident", (128, 128), BF16)
    emat = din("emat", (128, 128))
    tri = din("tri", (128, 128))
    t1m = din("t1m", (128, 128))
    l0125 = din("l0125", (128, 1))
    c128 = din("c128", (128, 1))
    onesr = din("onesr", (1, 128))
    onesbf = din("onesbf", (1, 128), BF16)
    onescolb = din("onescolb", (128, 1), BF16)
    epsc = din("epsc", (128, 1))
    cvr = din("cvr", (128, 64))
    rcpd = din("rcpd", (1, 1))
    negh = din("negh", (128, 1))

    yT = nc.dram_tensor("yT", [D, T], F32, kind="ExternalOutput").ap()
    x1f = nc.dram_tensor("x1f", [D, T], F32, kind="Internal").ap()
    taps = {}
    if debug:
        for nm, shape, dt in (
            ("tapQT", (128, 4096), BF16), ("tapKT", (128, 4096), BF16),
            ("tapV", (128, 4096), BF16), ("tapYT", (128, 4096), BF16),
            ("tapR1", (128, 4096), F32), ("tapX1", (128, 16384), BF16),
            ("tapU", (128, 32768), BF16), ("tapH1", (128, 32768), BF16),
            ("tapMD", (128, 4096), F32), ("tapWG", (128, 32), F32),
            ("tapSB", (128, 1024), BF16),
        ):
            taps[nm] = nc.dram_tensor(nm, list(shape), dt, kind="ExternalOutput").ap()

    with ExitStack() as ctx:
        e = ctx.enter_context
        AA = e(nc.sbuf_tensor("AA", [128, 32768], BF16))
        AB = e(nc.sbuf_tensor("AB", [128, 32768], BF16))
        X1T = e(nc.sbuf_tensor("X1T", [128, 16384], BF16))
        PB = [e(nc.sbuf_tensor(f"PB{i}", [128, 4096], BF16)) for i in range(2)]
        XRS = e(nc.sbuf_tensor("XRS", [128, 1536], F32))
        TMPA = e(nc.sbuf_tensor("TMPA", [128, 1024], F32))
        RETT = e(nc.sbuf_tensor("RETT", [128, 256], BF16))
        STB = e(nc.sbuf_tensor("STB", [1, 2048], BF16))
        SF = e(nc.sbuf_tensor("SF", [128, 1024], F32))
        SB = e(nc.sbuf_tensor("SB", [128, 1024], BF16))
        VBr = e(nc.sbuf_tensor("VBr", [1, D], BF16))
        LOGG = e(nc.sbuf_tensor("LOGG", [128, 16], F32))
        GCC = e(nc.sbuf_tensor("GCC", [128, 16], F32))
        GS1 = e(nc.sbuf_tensor("GS1", [128, 16], F32))
        QKB = e(nc.sbuf_tensor("QKB", [128, 16], F32))
        OUTB = e(nc.sbuf_tensor("OUTB", [128, 8], F32))
        S1C = e(nc.sbuf_tensor("S1C", [128, 32], F32))
        S2C = e(nc.sbuf_tensor("S2C", [128, 32], F32))
        S1HC = e(nc.sbuf_tensor("S1HC", [128, 32], F32))
        S2HC = e(nc.sbuf_tensor("S2HC", [128, 32], F32))
        S3C = e(nc.sbuf_tensor("S3C", [128, 8], F32))
        IDENT = e(nc.sbuf_tensor("IDENT", [128, 128], BF16))
        EMAT = e(nc.sbuf_tensor("EMAT", [128, 128], F32))
        TRI = e(nc.sbuf_tensor("TRI", [128, 128], F32))
        T1M = e(nc.sbuf_tensor("T1M", [128, 128], F32))
        CVR = e(nc.sbuf_tensor("CVR", [128, 64], F32))
        QTILF = e(nc.sbuf_tensor("QTILF", [128, 1024], BF16))
        VTILF = e(nc.sbuf_tensor("VTILF", [128, 1024], BF16))
        KTOK2 = e(nc.sbuf_tensor("KTOK2", [128, 256], BF16))
        L0125 = e(nc.sbuf_tensor("L0125", [128, 1], F32))
        C128 = e(nc.sbuf_tensor("C128", [128, 1], F32))
        ONESR = e(nc.sbuf_tensor("ONESR", [1, 128], F32))
        ONESBF = e(nc.sbuf_tensor("ONESBF", [1, 128], BF16))
        ONESCB = e(nc.sbuf_tensor("ONESCB", [128, 1], BF16))
        EPSC = e(nc.sbuf_tensor("EPSC", [128, 1], F32))
        RCPD = e(nc.sbuf_tensor("RCPD", [1, 1], F32))
        NEGH = e(nc.sbuf_tensor("NEGH", [128, 1], F32))
        GAMR = e(nc.sbuf_tensor("GAMR", [128, 16], F32))
        GROW = e(nc.sbuf_tensor("GROW", [128, 16], F32))

        PS = [e(nc.psum_tensor(f"PS{i}", [128, 512], F32)) for i in range(6)]
        PK = [e(nc.psum_tensor(f"PK{i}", [128, 1024], BF16)) for i in range(2)]

        sems = {nm: e(nc.semaphore("sem_" + nm)) for nm in ALLSEMS}

        P = Prog()

        AAa = AA[:]
        ABa = AB[:]
        AAf = AAa.bitcast(F32)
        ABf = ABa.bitcast(F32)

        def xtb(c, tt0, tl):
            return AAa[:, c * 2048 + tt0: c * 2048 + tt0 + tl]

        def qtw(c):
            return AAa[:, 16384 + c * 512: 16384 + c * 512 + 512]

        def ktw(c):
            return AAa[:, 20480 + c * 512: 20480 + c * 512 + 512]

        def qhead(h, cl):
            p0 = (h % 2) * 64
            o = 16384 + (h // 2) * 512 + cl * 128
            return AAa[p0:p0 + 64, o: o + 128]

        def khead(h, cl):
            p0 = (h % 2) * 64
            o = 20480 + (h // 2) * 512 + cl * 128
            return AAa[p0:p0 + 64, o: o + 128]

        def vslice(tt, i0, il):
            return AAa[:, 24576 + tt * 1024 + i0: 24576 + tt * 1024 + i0 + il]

        def ytw(c):
            return AAa[:, 28672 + c * 512: 28672 + c * 512 + 512]

        def yhead(h, cl):
            p0 = (h % 2) * 64
            o = 28672 + (h // 2) * 512 + cl * 128
            return AAa[p0:p0 + 64, o: o + 128]

        def mmask(h):
            return ABf[:, h * 128: h * 128 + 128]

        def dfull3d():
            return ABf[:, 2048:3072].rearrange("p (c t) -> p c t", c=8)

        def wupfull():
            return ABf[:, 3072:4096]

        def qtilf(h, cl):
            p0 = (h % 2) * 64
            o = (h // 2) * 128
            return QTILF[p0:p0 + 64, o: o + 128]

        def vtilf(h, cl):
            return VTILF[:, h * 64: h * 64 + 64]

        def ktok2(cqpar, h):
            return KTOK2[:, cqpar * 128 + (h % 2) * 64: cqpar * 128 + (h % 2) * 64 + 64]

        def r1w(c):
            return ABf[:, 4096 + c * 512: 4096 + c * 512 + 512]

        def xfw(c):
            return ABf[:, 8192 + c * 512: 8192 + c * 512 + 512]

        def rbw(c):
            return ABa[:, 16384 + c * 512: 16384 + c * 512 + 512]

        def sqbw(c):
            return ABa[:, 20480 + c * 512: 20480 + c * 512 + 512]

        def outwb(oc, k0, kl):
            return ABa[:, 24576 + oc * 1024 + k0: 24576 + oc * 1024 + k0 + kl]

        def ub(oc, tt0, tl):
            return ABa[:, oc * 1024 + tt0: oc * 1024 + tt0 + tl]

        def h1b(oc, tt0, tl):
            return AAa[:, oc * 1024 + tt0: oc * 1024 + tt0 + tl]

        def h2b(oc, tt0, tl):
            return ABa[:, oc * 1024 + tt0: oc * 1024 + tt0 + tl]

        def r2b(oc, tt0, tl):
            return AAf[:, oc * 1024 + tt0: oc * 1024 + tt0 + tl]

        def outf(oc):
            return AAf[:, 8192 + oc * 512: 8192 + oc * 512 + 512]

        def xf2(oc):
            return AAf[:, 12288 + oc * 512: 12288 + oc * 512 + 512]

        def rb2(oc):
            return AAa[:, 24576 + oc * 512: 24576 + oc * 512 + 512]

        def sqb2(oc):
            return AAa[:, 28672 + oc * 512: 28672 + oc * 512 + 512]

        def x1t(c, tt0, tl):
            return X1T[:, c * 2048 + tt0: c * 2048 + tt0 + tl]

        def atil(par):
            return RETT[:, par * 128: par * 128 + 128]

        def sfh(ping, h):
            p0 = (h % 2) * 64
            return SF[p0:p0 + 64, ping * 512 + (h // 2) * 64: ping * 512 + (h // 2) * 64 + 64]

        def sbh(ping, h):
            p0 = (h % 2) * 64
            return SB[p0:p0 + 64, ping * 512 + (h // 2) * 64: ping * 512 + (h // 2) * 64 + 64]

        def xrs(sl):
            return XRS[:, sl * 512: sl * 512 + 512]

        def outslot(idx):
            sl = idx % 5
            if sl < 3:
                return XRS[:, sl * 512: sl * 512 + 512]
            if sl == 3:
                return QTILF[:].bitcast(F32)
            return VTILF[:].bitcast(F32)

        def tmpa(hh):
            return TMPA[:, hh * 512: hh * 512 + 512]

        ST0 = TMPA[0:1, 0:512]
        ST1 = TMPA[0:1, 512:1024]

        for dst, src in (
            (QKB[:], qkb), (OUTB[:], outb), (S1C[:], s1c), (S2C[:], s2c),
            (S1HC[:], s1hc), (S2HC[:], s2hc), (S3C[:], s3c),
            (IDENT[:], ident), (EMAT[:], emat), (TRI[:], tri), (T1M[:], t1m),
            (L0125[:], l0125), (C128[:], c128),
            (ONESR[:], onesr), (ONESBF[:], onesbf), (ONESCB[:], onescolb),
            (EPSC[:], epsc), (RCPD[:], rcpd), (NEGH[:], negh), (GAMR[:], gam), (CVR[:], cvr),
        ):
            P.emit("q", lambda h, d=dst, s=src: h.dma_start(out=d, in_=s), 16, sem="qi")
        q_small = P.cnt["qi"]

        P.emit("g", lambda h: h.dma_start(out=VBr[:], in_=vb), 16, sem="gi")
        P.emit("g", lambda h: h.dma_start(
            out=AAa[:, 0:16384].rearrange("p (c t) -> p c t", c=8),
            in_=xT.rearrange("(c p) t -> p c t", p=128)), 16, sem="gi")
        for i in range(2):
            P.emit("g", lambda h, i=i: h.dma_start(
                out=ABa[:, 24576 + i * 4096: 24576 + (i + 1) * 4096], in_=outw[i]), 16, sem="gi")
        g_init = P.cnt["gi"]

        P.wait("s", "qi", q_small)
        P.emit("s", lambda h: h.activation(GROW[:], GAMR[:], AF.Sigmoid), 1)
        P.wait_self_dma("s", "s", P.cnt["s"])
        P.emit("s", lambda h: h.activation(LOGG[:], GROW[:], AF.Ln), 1)
        s_lg = P.cnt["s"]
        P.wait("v", "s", s_lg)
        P.wait("v", "qi", q_small)
        for h_ in range(H):
            P.emit("v", lambda h, hh=h_: h.tensor_scalar(
                ABf[:, 4096 + hh * 128: 4096 + hh * 128 + 128], EMAT[:],
                LOGG[:, hh:hh + 1], None, ALU.mult), 1 if h_ == H - 1 else 0)
        v_mscr = P.cnt["v"]
        P.wait("s", "v", v_mscr)
        P.emit("s", lambda h: h.activation(ABf[:, 6144:8192], ABf[:, 4096:6144], AF.Exp), 1)
        s_mexp = P.cnt["s"]
        P.wait("v", "s", s_mexp)
        for h_ in range(H):
            P.emit("v", lambda h, hh=h_: h.tensor_tensor(
                mmask(hh), ABf[:, 6144 + hh * 128: 6144 + hh * 128 + 128], TRI[:], ALU.mult))
        for cq in range(8):
            for half in range(2):
                hh_ = 2 * cq + half
                P.emit("v", lambda h, cq=cq, half=half, hh_=hh_: h.tensor_scalar(
                    ABf[half * 64:half * 64 + 64, 4096 + cq * 128: 4096 + cq * 128 + 128],
                    T1M[half * 64:half * 64 + 64, :],
                    LOGG[half * 64:half * 64 + 64, hh_:hh_ + 1], None, ALU.mult))
        for h_ in range(H):
            P.emit("v", lambda h, hh=h_: h.tensor_scalar(
                ABf[:, 5120 + hh * 64: 5120 + hh * 64 + 64], CVR[:],
                LOGG[:, hh:hh + 1], None, ALU.mult), 1 if h_ == H - 1 else 0)
        v_dscr = P.cnt["v"]
        P.wait("s", "v", v_dscr)
        P.emit("s", lambda h: h.activation(ABf[:, 2048:3072], ABf[:, 4096:5120], AF.Exp))
        P.emit("s", lambda h: h.activation(ABf[:, 3072:4096], ABf[:, 5120:6144], AF.Exp, bias=L0125[:]), 1)
        P.emit("v", lambda h: h.tensor_scalar(GS1[:], LOGG[:], C128[:], None, ALU.mult), 1)
        v_gs2 = P.cnt["v"]
        P.wait("s", "v", v_gs2)
        P.emit("s", lambda h: h.activation(GCC[:], GS1[:], AF.Exp), 1)
        P.emit("v", lambda h: h.memset(SF[:], 0.0))
        P.emit("v", lambda h: h.memset(SB[:], 0.0), 1)
        s_init = P.cnt["s"]
        v_init = P.cnt["v"]

        if debug:
            P.wait("q", "s", s_init)
            P.wait("q", "v", v_init)
            P.emit("q", lambda h: h.dma_start(out=taps["tapMD"], in_=ABf[:, 0:4096]), 16, sem="qi")
            P.emit("q", lambda h: h.dma_start(out=taps["tapWG"][:, 16:32], in_=GCC[:]), 16, sem="qi")

        gslot = [0]
        pb_free_t = {}
        ret_info, w_of = [], []
        mm1_cnt, v1_cnt, back_cnt, s1_cnt, v4_cnt = {}, {}, {}, {}, {}
        s_pk, sb_cast, prep_cnt = {}, {}, {}
        qk_evac = {}
        v_evac = {}
        ret_v_done = {}
        ret_s_done = {}
        op_t = {}
        opv_last = {}
        ln1_v = {}
        x1_spill_q = {-1: 0}

        def gdma(fn):
            key = gslot[0]
            if key >= 2:
                P.wait("g", "t", pb_free_t.get(key - 2, 0))
            sl = key % 2
            gslot[0] += 1
            cnt = P.emit("g", lambda h, sl=sl, fn=fn: fn(h, sl), 16, sem="g%d" % sl)
            return key, sl, cnt

        def layer_norm(P, reg_in, reg_out, reg_scr, stats_src, n_sl, t_prewait=None, bulk=None, scr2=None, banks=None, strows=None, prev=None):
            rb, sqb = stats_src
            bk0, bk1 = banks if banks is not None else (PS[2][:], PS[3][:])
            st0, st1 = strows if strows is not None else (ST0, ST1)
            prev_tstats, prev_varf, prev_tbc, prev_apply = prev if prev is not None else (0, 0, 0, 0)
            P.wait("s", "t", prev_tstats)
            if bulk is not None:
                bin_, brb, bsqb = bulk
                P.emit("s", lambda h: h.copy(brb, bin_))
                P.emit("s", lambda h: h.activation(bsqb, bin_, AF.Square), 1)
            else:
                for oc in range(8):
                    P.emit("s", lambda h, oc=oc: h.copy(rb(oc), reg_in(oc)))
                    P.emit("s", lambda h, oc=oc: h.activation(sqb(oc), reg_in(oc), AF.Square),
                           1 if oc == 7 else 0)
            s_stat = P.cnt["s"]
            P.wait("t", "s", s_stat)
            P.wait("t", "v", prev_varf)
            for tv in t_prewait or []:
                P.wait("t", tv[0], tv[1])
            for oc in range(8):
                P.emit("t", lambda h, oc=oc, bk0=bk0: h.matmul(bk0[0:1, 0:512], ONESCB[:], rb(oc),
                                                      start=(oc == 0), stop=(oc == 7)), 1)
            for oc in range(8):
                P.emit("t", lambda h, oc=oc, bk1=bk1: h.matmul(bk1[0:1, 0:512], ONESCB[:], sqb(oc),
                                                      start=(oc == 0), stop=(oc == 7)), 1)
            t_stats = P.cnt["t"]
            P.wait("v", "t", t_stats)
            P.wait("v", "t", prev_tbc)
            P.emit("v", lambda h, bk0=bk0, st0=st0: h.tensor_scalar(st0, bk0[0:1, 0:512], RCPD[:], None, ALU.mult))
            P.emit("v", lambda h, bk1=bk1, st1=st1: h.tensor_scalar(st1, bk1[0:1, 0:512], RCPD[:], None, ALU.mult), 1)
            P.wait_self_dma("v", "v", P.cnt["v"])
            P.emit("v", lambda h, st0=st0: h.tensor_copy(STB[0:1, 0:512], st0))
            P.emit("v", lambda h, st1=st1: h.tensor_copy(STB[0:1, 1024:1536], st1), 1)
            P.wait_self_dma("v", "v", P.cnt["v"])
            P.emit("v", lambda h, st0=st0: h.tensor_sub(XRS[0:1, 0:512], st0, STB[0:1, 0:512]))
            P.emit("v", lambda h, st1=st1: h.tensor_sub(XRS[0:1, 512:1024], st1, STB[0:1, 1024:1536]), 1)
            P.wait_self_dma("v", "v", P.cnt["v"])
            P.emit("v", lambda h: h.tensor_copy(STB[0:1, 512:1024], XRS[0:1, 0:512]))
            P.emit("v", lambda h: h.tensor_copy(STB[0:1, 1536:2048], XRS[0:1, 512:1024]), 1)
            v_st = P.cnt["v"]
            P.wait("t", "v", v_st)
            P.emit("t", lambda h, bk0=bk0: h.matmul(bk0, ONESBF[:], STB[0:1, 0:512], start=True, stop=False))
            P.emit("t", lambda h, bk0=bk0: h.matmul(bk0, ONESBF[:], STB[0:1, 512:1024], start=False, stop=True), 1)
            P.emit("t", lambda h, bk1=bk1: h.matmul(bk1, ONESBF[:], STB[0:1, 1024:1536], start=True, stop=False))
            P.emit("t", lambda h, bk1=bk1: h.matmul(bk1, ONESBF[:], STB[0:1, 1536:2048], start=False, stop=True), 1)
            t_bc = P.cnt["t"]
            mb, msq = scr2 if scr2 is not None else (tmpa(0), tmpa(1))
            P.wait("v", "t", t_bc)
            P.emit("v", lambda h, mb=mb, bk0=bk0: h.tensor_copy(mb, bk0), 1)
            P.wait_self_dma("v", "v", P.cnt["v"])
            P.emit("v", lambda h, mb=mb, msq=msq: h.tensor_mul(msq, mb, mb), 1)
            P.wait_self_dma("v", "v", P.cnt["v"])
            P.emit("v", lambda h, msq=msq, bk1=bk1: h.tensor_sub(xrs(0), bk1, msq), 1)
            v_varf = P.cnt["v"]
            P.wait("s", "v", v_varf)
            P.wait("s", "v", prev_apply)
            P.emit("s", lambda h: h.activation(xrs(1), xrs(0), AF.Ln, bias=EPSC[:]), 1)
            P.wait_self_dma("s", "s", P.cnt["s"])
            P.emit("s", lambda h: h.activation(xrs(2), xrs(1), AF.Exp, scale=NEGH[:]), 1)
            s_sqrt = P.cnt["s"]
            P.wait("v", "s", s_sqrt)
            mbrs = SF[:, 0:512].bitcast(F32).bitcast(F32) if False else SF[:, 0:512]
            P.emit("v", lambda h, mb=mb, mbrs=mbrs: h.tensor_mul(mbrs, mb, xrs(2)), 1)
            t1 = [SF[:, 512:1024], scr2[1], scr2[0]]
            P.wait_self_dma("v", "v", P.cnt["v"])
            mval, sval = {}, {}
            for oc in range(8):
                if oc >= 3:
                    P.wait_self_dma("v", "v", sval[oc - 3])
                mval[oc] = P.emit("v", lambda h, oc=oc, t1=t1: h.tensor_mul(
                    t1[oc % 3], reg_in(oc), xrs(2)), 1)
                if oc >= 1:
                    P.wait_self_dma("v", "v", mval[oc - 1])
                    sval[oc - 1] = P.emit("v", lambda h, oc=oc, t1=t1, mbrs=mbrs: h.tensor_sub(
                        reg_out(oc - 1), t1[(oc - 1) % 3], mbrs), 1)
            P.wait_self_dma("v", "v", mval[7])
            P.emit("v", lambda h, t1=t1, mbrs=mbrs: h.tensor_sub(reg_out(7), t1[7 % 3], mbrs), 1)
            return P.cnt["v"], v_varf

        ln1_varf = {-1: 0}
        ln1_cast_s = {-1: 0}
        ln1_sub_v = {-1: 0}
        ln1_tmp = {}

        def ln1_stage_a(w):
            P.wait("s", "v", opv_last[w])
            P.emit("s", lambda h: h.copy(ABa[:, 16384:20480], ABf[:, 4096:8192]))
            P.emit("s", lambda h: h.activation(ABa[:, 20480:24576], ABf[:, 4096:8192], AF.Square), 1)
            ln1_tmp[("s_stat", w)] = P.cnt["s"]

        def ln1_stage_b(w):
            bk0, bk1 = PS[2][:], PS[3][:]
            st0, st1 = ST0, ST1
            prev_tbc = 0
            P.wait("t", "s", ln1_tmp[("s_stat", w)])
            P.wait("t", "s", v_evac.get((w + 1, 7), v_evac[(w, 7)]))
            for oc in range(8):
                P.emit("t", lambda h, oc=oc: h.matmul(PS[2][0:1, 0:512], ONESCB[:], rbw(oc),
                                                      start=(oc == 0), stop=(oc == 7)), 1)
            for oc in range(8):
                P.emit("t", lambda h, oc=oc: h.matmul(PS[3][0:1, 0:512], ONESCB[:], sqbw(oc),
                                                      start=(oc == 0), stop=(oc == 7)), 1)
            t_stats = P.cnt["t"]
            P.wait("v", "t", t_stats)
            P.wait("v", "t", prev_tbc)
            P.emit("v", lambda h, bk0=bk0, st0=st0: h.tensor_scalar(st0, bk0[0:1, 0:512], RCPD[:], None, ALU.mult))
            P.emit("v", lambda h, bk1=bk1, st1=st1: h.tensor_scalar(st1, bk1[0:1, 0:512], RCPD[:], None, ALU.mult), 1)
            P.wait_self_dma("v", "v", P.cnt["v"])
            P.emit("v", lambda h, st0=st0: h.tensor_copy(STB[0:1, 0:512], st0))
            P.emit("v", lambda h, st1=st1: h.tensor_copy(STB[0:1, 1024:1536], st1), 1)
            P.wait_self_dma("v", "v", P.cnt["v"])
            P.emit("v", lambda h, st0=st0: h.tensor_sub(XRS[0:1, 0:512], st0, STB[0:1, 0:512]))
            P.emit("v", lambda h, st1=st1: h.tensor_sub(XRS[0:1, 512:1024], st1, STB[0:1, 1024:1536]), 1)
            P.wait_self_dma("v", "v", P.cnt["v"])
            P.emit("v", lambda h: h.tensor_copy(STB[0:1, 512:1024], XRS[0:1, 0:512]))
            P.emit("v", lambda h: h.tensor_copy(STB[0:1, 1536:2048], XRS[0:1, 512:1024]), 1)
            v_st = P.cnt["v"]
            P.wait("t", "v", v_st)
            P.emit("t", lambda h, bk0=bk0: h.matmul(bk0, ONESBF[:], STB[0:1, 0:512], start=True, stop=False))
            P.emit("t", lambda h, bk0=bk0: h.matmul(bk0, ONESBF[:], STB[0:1, 512:1024], start=False, stop=True), 1)
            P.emit("t", lambda h, bk1=bk1: h.matmul(bk1, ONESBF[:], STB[0:1, 1024:1536], start=True, stop=False))
            P.emit("t", lambda h, bk1=bk1: h.matmul(bk1, ONESBF[:], STB[0:1, 1536:2048], start=False, stop=True), 1)
            ln1_tmp[("t_bc", w)] = P.cnt["t"]

        def ln1_stage_c1(w):
            bk0, bk1 = PS[2][:], PS[3][:]
            P.wait("v", "t", ln1_tmp[("t_bc", w)])
            P.emit("v", lambda h: h.tensor_copy(tmpa(0), PS[2][:]), 1)
            P.wait_self_dma("v", "v", P.cnt["v"])
            P.emit("v", lambda h: h.tensor_mul(tmpa(1), tmpa(0), tmpa(0)), 1)
            P.wait_self_dma("v", "v", P.cnt["v"])
            P.emit("v", lambda h: h.tensor_sub(xrs(0), PS[3][:], tmpa(1)), 1)
            v_vf = P.cnt["v"]
            ln1_varf[w] = v_vf
            P.wait("s", "v", v_vf)
            P.emit("s", lambda h: h.activation(xrs(1), xrs(0), AF.Ln, bias=EPSC[:]), 1)
            P.wait_self_dma("s", "s", P.cnt["s"])
            P.emit("s", lambda h: h.activation(xrs(2), xrs(1), AF.Exp, scale=NEGH[:]), 1)
            ln1_tmp[("s_rs", w)] = P.cnt["s"]

        def ln1_stage_c2(w):
            P.wait("v", "s", ln1_tmp[("s_rs", w)])
            for oc in range(8):
                P.emit("v", lambda h, oc=oc: h.tensor_sub(xfw(oc), r1w(oc), tmpa(0)),
                       1 if oc == 7 else 0)
            ln1_sub_v[w] = P.cnt["v"]

        def ln1_stage_c3(w):
            t0 = w * TW
            P.wait_self_dma("v", "v", ln1_sub_v[w])
            for oc in range(8):
                P.emit("v", lambda h, oc=oc: h.tensor_mul(r1w(oc), xfw(oc), xrs(2)),
                       1 if oc == 7 else 0)
            v_x1 = P.cnt["v"]
            ln1_v[w] = v_x1
            P.wait("s", "v", v_x1)
            for oc in range(8):
                P.emit("s", lambda h, oc=oc, t0=t0: h.copy(x1t(oc, t0, TW), r1w(oc)),
                       1 if oc == 7 else 0)
            ln1_cast_s[w] = P.cnt["s"]

        def ln1_stage_c(w):
            ln1_stage_c1(w)
            ln1_stage_c2(w)
            ln1_stage_c3(w)

        for w in range(NW):
            t0 = w * TW
            wkeys = {}

            def wdma(idx, wkeys=None):
                if idx < 4:
                    fn = lambda h, sl, gi=idx: h.dma_start(out=PB[sl][:], in_=wqk[gi])
                else:
                    fn = lambda h, sl, vi=idx - 4: h.dma_start(out=PB[sl][:], in_=wvr[vi])
                wkeys[idx] = gdma(fn)

            wdma(0, wkeys)
            wdma(1, wkeys)
            if w > 0:
                ln1_stage_a(w - 1)

            prev_ret_v = ret_v_done.get(w - 1, 0)
            prev_op_t = op_t.get(w - 1, 0)
            prev_opv = opv_last.get(w - 1, 0)

            for ot in range(16):
                gi = ot // 4
                key, sl, gcnt = wkeys[gi]
                bank = ot % 4
                P.wait("t", "g%d" % sl, gcnt)
                if w == 0 and ot == 0:
                    P.wait("t", "v", v_init)
                    P.wait("t", "s", s_init)
                    P.wait("t", "gi", g_init)
                if ot < 4:
                    P.wait("t", "v", prev_opv)
                else:
                    P.wait("t", "v", qk_evac[(w, ot - 4)])
                for c in range(8):
                    P.emit("t", lambda h, ot=ot, c=c, sl=sl, bank=bank, t0=t0:
                           h.matmul(PS[bank][:],
                                    PB[sl][:, (ot % 4) * 1024 + c * 128:(ot % 4) * 1024 + c * 128 + 128],
                                    xtb(c, t0, TW), start=(c == 0), stop=(c == 7)), 1)
                pb_free_t[key] = P.cnt["t"]
                t_tile = P.cnt["t"]
                P.wait("v", "t", t_tile)
                if w > 0:
                    P.wait("v", "t", prev_op_t)
                dst = qtw(ot % 8) if ot < 8 else ktw(ot % 8)
                P.emit("v", lambda h, dst=dst, bank=bank, ot=ot:
                       h.tensor_scalar(dst, PS[bank][:], QKB[:, ot:ot + 1], None, ALU.add), 1)
                qk_evac[(w, ot)] = P.cnt["v"]
                if ot % 4 == 3 and ot // 4 + 2 <= 5:
                    wdma(ot // 4 + 2, wkeys)

            for tt in range(4):
                for ih in range(2):
                    bank = (tt * 2 + ih) % 4
                    P.wait("t", "v", qk_evac[(w, 12 + bank)])
                    if tt >= 2:
                        P.wait("t", "s", v_evac[(w, (tt - 2) * 2 + ih)])
                    for cc in range(8):
                        gk = wkeys[4 + cc // 4]
                        P.wait("t", "g%d" % gk[1], gk[2])
                        P.emit("t", lambda h, tt=tt, ih=ih, cc=cc, bank=bank, sl=gk[1], t0=t0:
                               h.matmul(PS[bank][:], xtb(cc, t0 + tt * 128, 128),
                                        PB[sl][:, (cc % 4) * 1024 + ih * 512:(cc % 4) * 1024 + ih * 512 + 512],
                                        start=(cc == 0), stop=False), 1)
                    P.wait("t", "gi", g_init)
                    P.emit("t", lambda h, ih=ih, bank=bank:
                           h.matmul(PS[bank][:], ONESBF[:], VBr[0:1, ih * 512: ih * 512 + 512],
                                    start=False, stop=True), 1)
                    t_vt = P.cnt["t"]
                    P.wait("s", "t", t_vt)
                    if w > 0:
                        P.wait("s", "t", prev_op_t)
                    P.emit("s", lambda h, tt=tt, ih=ih, bank=bank:
                           h.copy(vslice(tt, ih * 512, 512), PS[bank][:]), 1)
                    v_evac[(w, tt * 2 + ih)] = P.cnt["s"]
            pb_free_t[wkeys[4][0]] = P.cnt["t"]
            pb_free_t[wkeys[5][0]] = P.cnt["t"]

            if w > 0:
                ln1_stage_b(w - 1)

            if debug and w == 0:
                P.wait("q", "v", P.cnt["v"])
                P.emit("q", lambda h: h.dma_start(out=taps["tapQT"], in_=AAa[:, 16384:20480]), 16, sem="qi")
                P.emit("q", lambda h: h.dma_start(out=taps["tapKT"], in_=AAa[:, 20480:24576]), 16, sem="qi")
                P.emit("q", lambda h: h.dma_start(out=taps["tapV"], in_=AAa[:, 24576:28672]), 16, sem="qi")
                P.wait("v", "qi", P.cnt["qi"])
                P.wait("s", "qi", P.cnt["qi"])

            def emit_back(j):
                cl_j, h_j = ret_info[j]
                cglob_j = w_of[j] * 4 + cl_j
                par = j % 2
                p0 = (h_j % 2) * 64
                ping = cglob_j % 2
                prev = 1 - ping
                cqpar = (h_j // 2) % 2
                P.wait("t", "v", v1_cnt[j])
                P.wait("t", "v", prep_cnt[cglob_j])
                P.wait("t", "s", s_pk[(cglob_j * 8 + h_j // 2)])
                P.wait("t", "s", v_evac[(w_of[j], cl_j * 2 + h_j // 8)])
                if cglob_j > 0:
                    P.wait("t", "v", sb_cast[(cglob_j - 1, h_j // 8)])
                if j >= 2:
                    P.wait("t", "s", s1_cnt[j - 2])
                    P.wait("t", "v", v4_cnt[j - 2])
                P.emit("t", lambda h, h_j=h_j, cl_j=cl_j, par=par, prev=prev, p0=p0:
                       h.matmul(PS[par][p0:p0 + 64, 0:128], sbh(prev, h_j), qtilf(h_j, cl_j),
                                start=True, stop=False, tile_position=(p0, p0)), 1)
                P.emit("t", lambda h, h_j=h_j, cl_j=cl_j, par=par, p0=p0:
                       h.matmul(PS[par][p0:p0 + 64, 0:128], vslice(cl_j, h_j * 64, 64), atil(par),
                                start=False, stop=True, tile_position=(0, p0)), 1)
                P.emit("t", lambda h, h_j=h_j, cl_j=cl_j, par=par, p0=p0, cqpar=cqpar:
                       h.matmul(PS[par][p0:p0 + 64, 128:192], ktok2(cqpar, h_j), vtilf(h_j, cl_j),
                                start=True, stop=True, tile_position=(0, p0)), 1)
                back_cnt[j] = P.cnt["t"]
                P.wait("s", "t", back_cnt[j])
                P.emit("s", lambda h, h_j=h_j, cl_j=cl_j, par=par, p0=p0:
                       h.copy(yhead(h_j, cl_j), PS[par][p0:p0 + 64, 0:128]), 1)
                s1_cnt[j] = P.cnt["s"]
                P.wait("v", "t", back_cnt[j])
                P.wait("v", "s", s1_cnt[j])
                P.emit("v", lambda h, h_j=h_j, par=par, ping=ping, prev=prev, p0=p0:
                       h.scalar_tensor_tensor(sfh(ping, h_j), sfh(prev, h_j),
                                              GCC[p0:p0 + 64, h_j:h_j + 1],
                                              PS[par][p0:p0 + 64, 128:192],
                                              ALU.mult, ALU.add), 1)
                v4_cnt[j] = P.cnt["v"]
                if h_j in (7, 15):
                    half = h_j // 8
                    P.wait_self_dma("v", "v", P.cnt["v"])
                    P.emit("v", lambda h, ping=ping, half=half: h.tensor_copy(
                        SB[:, ping * 512 + half * 256: ping * 512 + half * 256 + 256],
                        SF[:, ping * 512 + half * 256: ping * 512 + half * 256 + 256]), 1)
                    sb_cast[(cglob_j, half)] = P.cnt["v"]

            for cl in range(4):
                cglob = w * 4 + cl
                for hh in range(H):
                    if cl == 3 and w > 0:
                        if hh == 0:
                            ln1_stage_c1(w - 1)
                        elif hh == 6:
                            ln1_stage_c2(w - 1)
                        elif hh == 12:
                            ln1_stage_c3(w - 1)
                    if hh == (1 if cglob > 0 else 0):
                        if cglob >= 1 and (cglob - 1) * 16 + 15 in back_cnt:
                            P.wait("v", "t", back_cnt[(cglob - 1) * 16 + 15])
                        P.wait("v", "s", v_evac[(w, cl * 2 + 1)])
                        P.emit("v", lambda h, cl=cl: h.tensor_tensor(
                            QTILF[:].rearrange("p (c t) -> p c t", c=8),
                            AAa[:, 16384:20480].rearrange("p (c t) -> p c t", c=8)[:, :, cl * 128: cl * 128 + 128],
                            dfull3d(), ALU.mult))
                        P.emit("v", lambda h, cl=cl: h.tensor_tensor(
                            VTILF[:],
                            AAa[:, 24576 + cl * 1024: 24576 + cl * 1024 + 1024],
                            wupfull(), ALU.mult), 1)
                        prep_cnt[cglob] = P.cnt["v"]
                    u = len(ret_info)
                    ret_info.append((cl, hh))
                    w_of.append(w)
                    par = u % 2
                    p0 = (hh % 2) * 64
                    P.wait("t", "v", qk_evac[(w, hh // 2)])
                    P.wait("t", "v", qk_evac[(w, 8 + hh // 2)])
                    if u >= 2:
                        P.wait("t", "v", v1_cnt[u - 2])
                    if hh % 2 == 0:
                        gpair = cglob * 8 + hh // 2
                        cqpar = (hh // 2) % 2
                        if gpair >= 2:
                            P.wait("t", "s", s_pk[gpair - 2])
                        P.emit("t", lambda h, hh=hh, cl=cl, cqpar=cqpar:
                               h.transpose(PK[cqpar][:, 0:128],
                                           AAa[:, 20480 + (hh // 2) * 512 + cl * 128:
                                               20480 + (hh // 2) * 512 + cl * 128 + 128],
                                           IDENT[:]), 1)
                        t_trans = P.cnt["t"]
                        P.wait("s", "t", t_trans)
                        if gpair >= 2 and (gpair - 2) * 2 + 1 in back_cnt:
                            P.wait("s", "t", back_cnt[(gpair - 2) * 2 + 1])
                        P.emit("s", lambda h, cqpar=cqpar:
                               h.copy(KTOK2[:, cqpar * 128: cqpar * 128 + 128],
                                      PK[cqpar][:, 0:128]), 1)
                        s_pk[gpair] = P.cnt["s"]
                    P.emit("t", lambda h, hh=hh, cl=cl, par=par, p0=p0:
                           h.matmul(PS[4 + par][:, 0:128], khead(hh, cl), qhead(hh, cl),
                                    start=True, stop=True, tile_position=(p0, 0)), 1)
                    mm1_cnt[u] = P.cnt["t"]
                    P.wait("v", "t", mm1_cnt[u])
                    P.emit("v", lambda h, hh=hh, par=par:
                           h.tensor_tensor(atil(par), PS[4 + par][:, 0:128], mmask(hh), ALU.mult), 1)
                    v1_cnt[u] = P.cnt["v"]
                    if u >= 1 and back_cnt.get(u - 1) is None:
                        emit_back(u - 1)
            emit_back(len(ret_info) - 1)
            ret_v_done[w] = P.cnt["v"]
            ret_s_done[w] = P.cnt["s"]

            if debug and w == 0:
                P.wait("q", "s", ret_s_done[0])
                P.wait("q", "v", ret_v_done[0])
                P.emit("q", lambda h: h.dma_start(out=taps["tapYT"], in_=AAa[:, 28672:32768]), 16, sem="qi")
                P.emit("q", lambda h: h.dma_start(out=taps["tapSB"], in_=SB[:]), 16, sem="qi")
                P.wait("s", "qi", P.cnt["qi"])
                P.wait("v", "qi", P.cnt["qi"])

            P.wait("t", "gi", g_init)
            opevac = {}
            for oc in range(8):
                bank = oc % 4
                if oc >= 3:
                    P.wait("q", "v", opevac[oc - 3])
                else:
                    P.wait("q", "v", ln1_v.get(w - 1, 0))
                P.emit("q", lambda h, oc=oc, t0=t0: h.dma_start(
                    out=xrs(oc % 3), in_=xT[oc * 128:(oc + 1) * 128, t0:t0 + TW]), 16,
                    sem="q%d" % (oc % 3))
                q_rr = ("q%d" % (oc % 3), P.cnt["q%d" % (oc % 3)])
                P.wait("t", "s", ret_s_done[w])
                P.wait("t", "v", ret_v_done[w])
                P.wait("t", "v", ln1_varf.get(w - 1, 0))
                if oc >= 4:
                    P.wait("t", "v", opevac[oc - 4])
                for c in range(8):
                    P.emit("t", lambda h, oc=oc, c=c, bank=bank:
                           h.matmul(PS[bank][:], outwb(oc, c * 128, 128), ytw(c),
                                    start=(c == 0), stop=(c == 7)), 1)
                t_tile = P.cnt["t"]
                P.wait("v", "t", t_tile)
                P.wait("v", q_rr[0], q_rr[1])
                P.wait("v", "s", ln1_cast_s.get(w - 1, 0))
                P.emit("v", lambda h, oc=oc, bank=bank:
                       h.scalar_tensor_tensor(r1w(oc), PS[bank][:], OUTB[:, oc:oc + 1],
                                              xrs(oc % 3), ALU.add, ALU.add), 1)
                opevac[oc] = P.cnt["v"]
            op_t[w] = P.cnt["t"]
            opv_last[w] = opevac[7]
            if debug and w == 0:
                P.wait("q", "v", opevac[7])
                P.emit("q", lambda h: h.dma_start(out=taps["tapR1"], in_=ABf[:, 4096:8192]), 16, sem="qi")
                P.wait("v", "qi", P.cnt["qi"])
                P.wait("s", "qi", P.cnt["qi"])
            if w == NW - 1:
                ln1_stage_a(w)
                ln1_stage_b(w)
                ln1_stage_c(w)

        if debug:
            P.wait("q", "s", P.cnt["s"])
            P.emit("q", lambda h: h.dma_start(out=taps["tapX1"], in_=X1T[:]), 16, sem="qi")

        att_s = P.cnt["s"]
        ffn_state = {"store_q": 0, "ln2_v": 0, "out_v": {}}
        store_q = 0
        ln2_v = 0
        prev_out_v = 0
        PKF = [PK[i][:].bitcast(F32) for i in range(2)]

        def do_ln2(tb, ths=(0, 1)):
            nonlocal store_q, ln2_v
            out_v_tb = ffn_state["out_v"][tb]
            prev_vout = ffn_state.get("ln2_pv", 0)
            prev_tstats = ffn_state.get("ln2_pt", 0)
            prev_varf = ffn_state.get("ln2_pvf", 0)
            for th in ths:
                P.wait("s", "v", out_v_tb)
                P.wait("s", "qs", store_q)
                v_out, _vf2 = layer_norm(
                    P,
                    (lambda oc, th=th: r2b(oc, th * 512, 512)),
                    outf, xf2, (rb2, sqb2), 3,
                    bulk=(AAf[:, 0:8192].rearrange("p (c t) -> p c t", c=8)[:, :, th * 512:(th + 1) * 512],
                          AAa[:, 24576:28672].rearrange("p (c t) -> p c t", c=8),
                          AAa[:, 28672:32768].rearrange("p (c t) -> p c t", c=8)),
                    scr2=(QTILF[:].bitcast(F32), VTILF[:].bitcast(F32)),
                    banks=(PKF[0][:, 0:512], PKF[1][:, 0:512]),
                    strows=(SF[0:1, 0:512], SF[0:1, 512:1024]),
                    prev=(prev_tstats, prev_varf, prev_tstats, prev_vout))
                prev_vout = v_out
                prev_varf = _vf2
                prev_tstats = P.cnt["t"]
                P.wait("q", "v", v_out)
                P.emit("q", lambda h, tb=tb, th=th: h.dma_start(
                    out=yT.rearrange("(c p) t -> p c t", p=128)[:, :, tb * TB + th * 512: tb * TB + th * 512 + 512],
                    in_=AAf[:, 8192:12288].rearrange("p (c t) -> p c t", c=8)), 16, sem="qs")
                store_q = P.cnt["qs"]
                P.wait("v", "qs", store_q)
                ffn_state["ln2_pv"] = prev_vout
                ffn_state["ln2_pt"] = prev_tstats
                ffn_state["ln2_pvf"] = prev_varf
            ln2_v = prev_vout

        for tb in range(2):
            tg0 = tb * TB
            uevac_v = {}
            uevac_s = {}
            vt_cons = {}
            vt = 0
            for ot in range(32):
                key, sl, gcnt = gdma(lambda h, sl, ot=ot: h.dma_start(out=PB[sl][:, 0:2048], in_=w12[ot]))
                P.wait("t", "g%d" % sl, gcnt)
                if tb == 0 and ot == 0:
                    P.wait("t", "s", ln1_cast_s[1])
                    P.wait("t", "v", opv_last[NW - 1])
                    P.wait("t", "v", ret_v_done[NW - 1])
                    P.wait("t", "s", ret_s_done[NW - 1])
                if tb == 1 and ot == 0:
                    P.wait("t", "v", prev_out_v)
                if tb == 1 and ot == 2:
                    do_ln2(0, ths=(0, 1))
                for wi in range(2):
                    pair = vt % 3
                    if tb == 0 and pair == 1 and vt < 3:
                        P.wait("t", "v", ln1_varf[NW - 1])
                    if vt >= 3:
                        eng, cval = vt_cons[vt - 3]
                        P.wait("t", eng, cval)
                    for c in range(8):
                        for th in range(2):
                            P.emit("t", lambda h, wi=wi, th=th, c=c, pair=pair, sl=sl, tg0=tg0:
                                   h.matmul(PS[pair * 2 + th][:],
                                            PB[sl][:, wi * 1024 + c * 128: wi * 1024 + c * 128 + 128],
                                            x1t(c, tg0 + th * 512, 512),
                                            start=(c == 0), stop=(c == 7)), 1)
                    t_u = P.cnt["t"]
                    if wi == 0:
                        P.wait("s", "t", t_u)
                        if ot >= 1:
                            P.wait("s", "v", uevac_v[ot - 1])
                        if tb == 0 and ot == 0:
                            P.wait("s", "v", ln1_sub_v[NW - 1])
                        for th in range(2):
                            P.emit("s", lambda h, ot=ot, th=th, pair=pair:
                                   h.activation(tmpa(th), PS[pair * 2 + th][:], AF.Silu,
                                                scale=S1C[:, ot:ot + 1]), 1)
                        uevac_s[ot] = P.cnt["s"]
                        vt_cons[vt] = ("s", uevac_s[ot])
                    else:
                        P.wait("v", "s", uevac_s[ot])
                        P.wait("v", "t", t_u)
                        if tb == 0 and ot == 8:
                            P.wait("v", "s", ln1_cast_s[NW - 1])
                        for th in range(2):
                            P.emit("v", lambda h, ot=ot, th=th, pair=pair:
                                   h.scalar_tensor_tensor(ub(ot, th * 512, 512), PS[pair * 2 + th][:],
                                                          S2C[:, ot:ot + 1], tmpa(th),
                                                          ALU.mult, ALU.mult), 1)
                        uevac_v[ot] = P.cnt["v"]
                        vt_cons[vt] = ("v", uevac_v[ot])
                    vt += 1
                pb_free_t[key] = P.cnt["t"]
            if debug and tb == 0:
                P.wait("q", "v", P.cnt["v"])
                P.emit("q", lambda h: h.dma_start(out=taps["tapU"], in_=ABa[:]), 16, sem="qi")
                P.wait("s", "qi", P.cnt["qi"])
                P.wait("v", "qi", P.cnt["qi"])
                P.wait("t", "qi", P.cnt["qi"])

            h1evac = {}
            for ot in range(32):
                key, sl, gcnt = gdma(lambda h, sl, ot=ot: h.dma_start(out=PB[sl][:], in_=h1p[ot]))
                P.wait("t", "g%d" % sl, gcnt)
                if ot == 0:
                    banks = (PKF[0][:, 0:512], PKF[1][:, 0:512])
                    P.wait("t", "v", ffn_state.get("ln2_pvf", 0))
                elif ot == 1:
                    banks = (PS[2][:], PS[3][:])
                    P.wait("t", "v", uevac_v[30])
                elif ot == 2:
                    banks = (PS[4][:], PS[5][:])
                    P.wait("t", "s", uevac_s[31])
                else:
                    banks = (PS[(ot % 3) * 2][:], PS[(ot % 3) * 2 + 1][:])
                    P.wait("t", "s", h1evac[ot - 3])
                for c in range(32):
                    P.wait("t", "v", uevac_v[c])
                    for th in range(2):
                        P.emit("t", lambda h, c=c, th=th, bk=banks[th], sl=sl:
                               h.matmul(bk, PB[sl][:, c * 128: c * 128 + 128],
                                        ub(c, th * 512, 512), start=(c == 0), stop=(c == 31)), 1)
                pb_free_t[key] = P.cnt["t"]
                t_h = P.cnt["t"]
                P.wait("s", "t", t_h)
                P.wait("s", "qs", store_q)
                for th in range(2):
                    P.emit("s", lambda h, ot=ot, th=th, bk=banks[th]:
                           h.activation(h1b(ot, th * 512, 512), bk, AF.Gelu,
                                        scale=S1HC[:, ot:ot + 1]), 1)
                h1evac[ot] = P.cnt["s"]
            if debug and tb == 0:
                P.wait("q", "s", P.cnt["s"])
                P.emit("q", lambda h: h.dma_start(out=taps["tapH1"], in_=AAa[:]), 16, sem="qi")
                P.wait("s", "qi", P.cnt["qi"])
                P.wait("v", "qi", P.cnt["qi"])
                P.wait("t", "qi", P.cnt["qi"])

            h2evac = {}
            for ot in range(32):
                key, sl, gcnt = gdma(lambda h, sl, ot=ot: h.dma_start(out=PB[sl][:], in_=h2p[ot]))
                P.wait("t", "g%d" % sl, gcnt)
                if ot == 0:
                    banks = (PKF[0][:, 0:512], PKF[1][:, 0:512])
                    P.wait("t", "s", h1evac[0])
                elif ot == 1:
                    banks = (PS[2][:], PS[3][:])
                    P.wait("t", "s", h1evac[31])
                elif ot == 2:
                    banks = (PS[4][:], PS[5][:])
                    P.wait("t", "s", h1evac[29])
                else:
                    banks = (PS[(ot % 3) * 2][:], PS[(ot % 3) * 2 + 1][:])
                    P.wait("t", "s", h2evac[ot - 3])
                    if ot == 3:
                        P.wait("t", "s", h1evac[30])
                for c in range(32):
                    P.wait("t", "s", h1evac[c])
                    for th in range(2):
                        P.emit("t", lambda h, c=c, th=th, bk=banks[th], sl=sl:
                               h.matmul(bk, PB[sl][:, c * 128: c * 128 + 128],
                                        h1b(c, th * 512, 512), start=(c == 0), stop=(c == 31)), 1)
                pb_free_t[key] = P.cnt["t"]
                t_h = P.cnt["t"]
                P.wait("s", "t", t_h)
                for th in range(2):
                    P.emit("s", lambda h, ot=ot, th=th, bk=banks[th]:
                           h.activation(h2b(ot, th * 512, 512), bk, AF.Gelu,
                                        scale=S2HC[:, ot:ot + 1]), 1)
                h2evac[ot] = P.cnt["s"]

            outevac = {}
            for ot in range(8):
                key, sl, gcnt = gdma(lambda h, sl, ot=ot: h.dma_start(out=PB[sl][:], in_=w3p[ot]))
                P.wait("t", "g%d" % sl, gcnt)
                if ot == 0:
                    banks = (PKF[0][:, 0:512], PKF[1][:, 0:512])
                    P.wait("t", "s", h2evac[0])
                elif ot == 1:
                    banks = (PS[2][:], PS[3][:])
                    P.wait("t", "s", h2evac[31])
                elif ot == 2:
                    banks = (PS[4][:], PS[5][:])
                    P.wait("t", "s", h2evac[29])
                else:
                    banks = (PS[(ot % 3) * 2][:], PS[(ot % 3) * 2 + 1][:])
                    P.wait("t", "v", outevac[ot - 3])
                    if ot == 3:
                        P.wait("t", "s", h2evac[30])
                for c in range(32):
                    P.wait("t", "s", h2evac[c])
                    for th in range(2):
                        P.emit("t", lambda h, c=c, th=th, bk=banks[th], sl=sl:
                               h.matmul(bk, PB[sl][:, c * 128: c * 128 + 128],
                                        h2b(c, th * 512, 512), start=(c == 0), stop=(c == 31)), 1)
                pb_free_t[key] = P.cnt["t"]
                t_o = P.cnt["t"]
                P.wait("v", "t", t_o)
                if ot == 0:
                    P.wait("v", "s", ln1_cast_s[NW - 1])
                for th in range(2):
                    P.emit("v", lambda h, ot=ot, th=th, bk=banks[th], tg0=tg0:
                           h.scalar_tensor_tensor(r2b(ot, th * 512, 512), bk,
                                                  S3C[:, ot:ot + 1], x1t(ot, tg0 + th * 512, 512),
                                                  ALU.mult, ALU.add), 1)
                outevac[ot] = P.cnt["v"]
            out_v = P.cnt["v"]
            prev_out_v = out_v
            ffn_state["out_v"][tb] = out_v

            if tb == 1:
                do_ln2(1, ths=(0, 1))

        P.wait_self_dma("q", "qs", P.cnt["qs"])
        P.emit("q", lambda h: h.nop())

        block = e(nc.Block())

        @block.tensor
        def _(t):
            P.run("t", t, sems)

        @block.vector
        def _(v):
            P.run("v", v, sems)

        @block.scalar
        def _(s):
            P.run("s", s, sems)

        @block.gpsimd
        def _(g):
            P.run("g", g, sems)

        @block.sync
        def _(q):
            P.run("q", q, sems)

    return nc


def _panels(w):
    O, I = w.shape
    no, ni = O // 128, I // 128
    return np.ascontiguousarray(w.reshape(no, 128, ni, 128).transpose(0, 3, 2, 1)).reshape(no, 128, I)


def _colvec(v):
    return np.ascontiguousarray(v.reshape(-1, 128).T)


def _group4(panels):
    n = panels.shape[0]
    return np.ascontiguousarray(panels.reshape(n // 4, 4, 128, 1024).transpose(0, 2, 1, 3)).reshape(n // 4, 128, 4096)


def kernel(**inputs):
    import ml_dtypes

    x = inputs["L0_attn_x"]
    wqkv = inputs["L0_attn_Wqkv_w"]
    bqkv = inputs["L0_attn_Wqkv_b"]

    qk_panels = _panels(wqkv[0:2048])
    wv = wqkv[2048:3072]
    wvr_c = np.ascontiguousarray(wv.reshape(1024, 8, 128).transpose(1, 2, 0))
    fin = dict(
        wqk=_group4(qk_panels),
        wvr=_group4(wvr_c),
        vb=np.ascontiguousarray(bqkv[2048:3072]).reshape(1, 1024),
        qkb=_colvec(bqkv[0:2048]),
        outw=_group4(_panels(inputs["L0_attn_out_w"])),
        outb=_colvec(inputs["L0_attn_out_b"]),
        w12=np.ascontiguousarray(
            np.concatenate([_panels(inputs["L0_ffn_w1_w"]), _panels(inputs["L0_ffn_w2_w"])], axis=2)),
        s1c=_colvec(inputs["L0_ffn_w1_s"]), s2c=_colvec(inputs["L0_ffn_w2_s"]),
        h1p=_panels(inputs["L0_ffn_h1_w"]), s1hc=_colvec(inputs["L0_ffn_h1_s"]),
        h2p=_panels(inputs["L0_ffn_h2_w"]), s2hc=_colvec(inputs["L0_ffn_h2_s"]),
        w3p=_panels(inputs["L0_ffn_w3_w"]), s3c=_colvec(inputs["L0_ffn_w3_s"]),
        gam=np.ascontiguousarray(np.broadcast_to(inputs["L0_attn_gamma_raw"].reshape(1, H), (128, H))),
        ident=np.eye(128, dtype=ml_dtypes.bfloat16),
        emat=np.maximum(np.arange(128)[None, :] - np.arange(128)[:, None], 0).astype(np.float32),
        tri=(0.125 * (np.arange(128)[:, None] <= np.arange(128)[None, :])).astype(np.float32),
        t1m=np.broadcast_to((np.arange(128) + 1).astype(np.float32), (128, 128)).copy(),
        l0125=np.full((128, 1), np.log(0.125), np.float32),
        c128=np.full((128, 1), 128.0, np.float32),
        onesr=np.ones((1, 128), np.float32),
        onesbf=np.ones((1, 128), ml_dtypes.bfloat16),
        onescolb=np.ones((128, 1), ml_dtypes.bfloat16),
        epsc=np.full((128, 1), 1e-5, np.float32),
        rcpd=np.full((1, 1), 1.0 / D, np.float32),
        negh=np.full((128, 1), -0.5, np.float32),
        cvr=np.broadcast_to((127.0 - np.arange(128)).astype(np.float32).reshape(128, 1), (128, 64)).copy(),
    )

    nc = build(debug=False)
    in_maps = []
    for b in range(NCORE):
        m = dict(fin)
        m["xT"] = np.ascontiguousarray(x[b].T)
        in_maps.append(m)
    res = run_bass_kernel_spmd(nc, in_maps, core_ids=list(range(NCORE)))
    out = np.stack([np.ascontiguousarray(res.results[b]["yT"].T) for b in range(NCORE)], axis=0)
    return out


# version 16 (final)
# speedup vs baseline: 1.0032x; 1.0032x over previous
import numpy as np
from contextlib import ExitStack

import concourse.bass as bass
import concourse.mybir as mybir
from concourse.bass_utils import run_bass_kernel_spmd

F32 = mybir.dt.float32
BF16 = mybir.dt.bfloat16
AF = mybir.ActivationFunctionType
ALU = mybir.AluOpType

D = 1024
T = 2048
H = 16
TW = 512
NW = T // TW
TB = 1024
NCORE = 8

ENGS = ("t", "v", "s", "g", "q")
DMASEMS = ("g0", "g1", "gi", "qi", "q0", "q1", "q2", "qs")
ALLSEMS = ENGS + DMASEMS


class Prog:
    def __init__(self):
        self.ops = {e: [] for e in ENGS}
        self.cnt = {e: 0 for e in ALLSEMS}
        self.lastw = {e: {f: 0 for f in ALLSEMS} for e in ENGS}

    def wait(self, eng, on, val):
        if val <= 0 or eng == on:
            return
        assert val <= self.cnt[on], (eng, on, val, self.cnt[on])
        if self.lastw[eng][on] >= val:
            return
        self.lastw[eng][on] = val
        self.ops[eng].append(("w", on, val))

    def wait_self_dma(self, eng, on, val):
        if val <= 0:
            return
        self.ops[eng].append(("w", on, val))

    def emit(self, eng, fn, inc=0, sem=None):
        tgt = sem if sem is not None else eng
        self.ops[eng].append(("o", fn, inc, tgt))
        if inc:
            self.cnt[tgt] += inc
        return self.cnt[tgt]

    def run(self, eng, h, sems):
        for item in self.ops[eng]:
            if item[0] == "w":
                h.wait_ge(sems[item[1]], item[2])
            else:
                ins = item[1](h)
                if item[2]:
                    ins.then_inc(sems[item[3]], item[2])


def build(debug=False):
    nc = bass.Bass("TRN2", target_bir_lowering=False)

    def din(name, shape, dtype=F32):
        return nc.dram_tensor(name, list(shape), dtype, kind="ExternalInput").ap()

    xT = din("xT", (D, T))
    wqk = din("wqk", (4, 128, 4096))
    wvr = din("wvr", (2, 128, 4096))
    vb = din("vb", (1, D))
    qkb = din("qkb", (128, 16))
    outw = din("outw", (2, 128, 4096))
    outb = din("outb", (128, 8))
    w12 = din("w12", (32, 128, 2048))
    s1c = din("s1c", (128, 32))
    s2c = din("s2c", (128, 32))
    h1p = din("h1p", (32, 128, 4096))
    s1hc = din("s1hc", (128, 32))
    h2p = din("h2p", (32, 128, 4096))
    s2hc = din("s2hc", (128, 32))
    w3p = din("w3p", (8, 128, 4096))
    s3c = din("s3c", (128, 8))
    gam = din("gam", (128, H))
    ident = din("ident", (128, 128), BF16)
    emat = din("emat", (128, 128))
    tri = din("tri", (128, 128))
    t1m = din("t1m", (128, 128))
    l0125 = din("l0125", (128, 1))
    c128 = din("c128", (128, 1))
    onesr = din("onesr", (1, 128))
    onesbf = din("onesbf", (1, 128), BF16)
    onescolb = din("onescolb", (128, 1), BF16)
    epsc = din("epsc", (128, 1))
    cvr = din("cvr", (128, 64))
    rcpd = din("rcpd", (1, 1))
    negh = din("negh", (128, 1))

    yT = nc.dram_tensor("yT", [D, T], F32, kind="ExternalOutput").ap()
    x1f = nc.dram_tensor("x1f", [D, T], F32, kind="Internal").ap()
    taps = {}
    if debug:
        for nm, shape, dt in (
            ("tapQT", (128, 4096), BF16), ("tapKT", (128, 4096), BF16),
            ("tapV", (128, 4096), BF16), ("tapYT", (128, 4096), BF16),
            ("tapR1", (128, 4096), F32), ("tapX1", (128, 16384), BF16),
            ("tapU", (128, 32768), BF16), ("tapH1", (128, 32768), BF16),
            ("tapMD", (128, 4096), F32), ("tapWG", (128, 32), F32),
            ("tapSB", (128, 1024), BF16),
        ):
            taps[nm] = nc.dram_tensor(nm, list(shape), dt, kind="ExternalOutput").ap()

    with ExitStack() as ctx:
        e = ctx.enter_context
        AA = e(nc.sbuf_tensor("AA", [128, 32768], BF16))
        AB = e(nc.sbuf_tensor("AB", [128, 32768], BF16))
        X1T = e(nc.sbuf_tensor("X1T", [128, 16384], BF16))
        PB = [e(nc.sbuf_tensor(f"PB{i}", [128, 4096], BF16)) for i in range(2)]
        XRS = e(nc.sbuf_tensor("XRS", [128, 1536], F32))
        TMPA = e(nc.sbuf_tensor("TMPA", [128, 1024], F32))
        RETT = e(nc.sbuf_tensor("RETT", [128, 256], BF16))
        STB = e(nc.sbuf_tensor("STB", [1, 2048], BF16))
        SF = e(nc.sbuf_tensor("SF", [128, 1024], F32))
        SB = e(nc.sbuf_tensor("SB", [128, 1024], BF16))
        VBr = e(nc.sbuf_tensor("VBr", [1, D], BF16))
        LOGG = e(nc.sbuf_tensor("LOGG", [128, 16], F32))
        GCC = e(nc.sbuf_tensor("GCC", [128, 16], F32))
        GS1 = e(nc.sbuf_tensor("GS1", [128, 16], F32))
        QKB = e(nc.sbuf_tensor("QKB", [128, 16], F32))
        OUTB = e(nc.sbuf_tensor("OUTB", [128, 8], F32))
        S1C = e(nc.sbuf_tensor("S1C", [128, 32], F32))
        S2C = e(nc.sbuf_tensor("S2C", [128, 32], F32))
        S1HC = e(nc.sbuf_tensor("S1HC", [128, 32], F32))
        S2HC = e(nc.sbuf_tensor("S2HC", [128, 32], F32))
        S3C = e(nc.sbuf_tensor("S3C", [128, 8], F32))
        IDENT = e(nc.sbuf_tensor("IDENT", [128, 128], BF16))
        EMAT = e(nc.sbuf_tensor("EMAT", [128, 128], F32))
        TRI = e(nc.sbuf_tensor("TRI", [128, 128], F32))
        T1M = e(nc.sbuf_tensor("T1M", [128, 128], F32))
        CVR = e(nc.sbuf_tensor("CVR", [128, 64], F32))
        QTILF = e(nc.sbuf_tensor("QTILF", [128, 1024], BF16))
        VTILF = e(nc.sbuf_tensor("VTILF", [128, 1024], BF16))
        KTOK2 = e(nc.sbuf_tensor("KTOK2", [128, 256], BF16))
        L0125 = e(nc.sbuf_tensor("L0125", [128, 1], F32))
        C128 = e(nc.sbuf_tensor("C128", [128, 1], F32))
        ONESR = e(nc.sbuf_tensor("ONESR", [1, 128], F32))
        ONESBF = e(nc.sbuf_tensor("ONESBF", [1, 128], BF16))
        ONESCB = e(nc.sbuf_tensor("ONESCB", [128, 1], BF16))
        EPSC = e(nc.sbuf_tensor("EPSC", [128, 1], F32))
        RCPD = e(nc.sbuf_tensor("RCPD", [1, 1], F32))
        NEGH = e(nc.sbuf_tensor("NEGH", [128, 1], F32))
        GAMR = e(nc.sbuf_tensor("GAMR", [128, 16], F32))
        GROW = e(nc.sbuf_tensor("GROW", [128, 16], F32))

        PS = [e(nc.psum_tensor(f"PS{i}", [128, 512], F32)) for i in range(6)]
        PK = [e(nc.psum_tensor(f"PK{i}", [128, 1024], BF16)) for i in range(2)]

        sems = {nm: e(nc.semaphore("sem_" + nm)) for nm in ALLSEMS}

        P = Prog()

        AAa = AA[:]
        ABa = AB[:]
        AAf = AAa.bitcast(F32)
        ABf = ABa.bitcast(F32)

        def xtb(c, tt0, tl):
            return AAa[:, c * 2048 + tt0: c * 2048 + tt0 + tl]

        def qtw(c):
            return AAa[:, 16384 + c * 512: 16384 + c * 512 + 512]

        def ktw(c):
            return AAa[:, 20480 + c * 512: 20480 + c * 512 + 512]

        def qhead(h, cl):
            p0 = (h % 2) * 64
            o = 16384 + (h // 2) * 512 + cl * 128
            return AAa[p0:p0 + 64, o: o + 128]

        def khead(h, cl):
            p0 = (h % 2) * 64
            o = 20480 + (h // 2) * 512 + cl * 128
            return AAa[p0:p0 + 64, o: o + 128]

        def vslice(tt, i0, il):
            return AAa[:, 24576 + tt * 1024 + i0: 24576 + tt * 1024 + i0 + il]

        def ytw(c):
            return AAa[:, 28672 + c * 512: 28672 + c * 512 + 512]

        def yhead(h, cl):
            p0 = (h % 2) * 64
            o = 28672 + (h // 2) * 512 + cl * 128
            return AAa[p0:p0 + 64, o: o + 128]

        def mmask(h):
            return ABf[:, h * 128: h * 128 + 128]

        def dfull3d():
            return ABf[:, 2048:3072].rearrange("p (c t) -> p c t", c=8)

        def wupfull():
            return ABf[:, 3072:4096]

        def qtilf(h, cl):
            p0 = (h % 2) * 64
            o = (h // 2) * 128
            return QTILF[p0:p0 + 64, o: o + 128]

        def vtilf(h, cl):
            return VTILF[:, h * 64: h * 64 + 64]

        def ktok2(cqpar, h):
            return KTOK2[:, cqpar * 128 + (h % 2) * 64: cqpar * 128 + (h % 2) * 64 + 64]

        def r1w(c):
            return ABf[:, 4096 + c * 512: 4096 + c * 512 + 512]

        def xfw(c):
            return ABf[:, 8192 + c * 512: 8192 + c * 512 + 512]

        def rbw(c):
            return ABa[:, 16384 + c * 512: 16384 + c * 512 + 512]

        def sqbw(c):
            return ABa[:, 20480 + c * 512: 20480 + c * 512 + 512]

        def outwb(oc, k0, kl):
            return ABa[:, 24576 + oc * 1024 + k0: 24576 + oc * 1024 + k0 + kl]

        def ub(oc, tt0, tl):
            return ABa[:, oc * 1024 + tt0: oc * 1024 + tt0 + tl]

        def h1b(oc, tt0, tl):
            return AAa[:, oc * 1024 + tt0: oc * 1024 + tt0 + tl]

        def h2b(oc, tt0, tl):
            return ABa[:, oc * 1024 + tt0: oc * 1024 + tt0 + tl]

        def r2b(oc, tt0, tl):
            return AAf[:, oc * 1024 + tt0: oc * 1024 + tt0 + tl]

        def outf(oc):
            return AAf[:, 8192 + oc * 512: 8192 + oc * 512 + 512]

        def xf2(oc):
            return AAf[:, 12288 + oc * 512: 12288 + oc * 512 + 512]

        def rb2(oc):
            return AAa[:, 24576 + oc * 512: 24576 + oc * 512 + 512]

        def sqb2(oc):
            return AAa[:, 28672 + oc * 512: 28672 + oc * 512 + 512]

        def x1t(c, tt0, tl):
            return X1T[:, c * 2048 + tt0: c * 2048 + tt0 + tl]

        def atil(par):
            return RETT[:, par * 128: par * 128 + 128]

        def sfh(ping, h):
            p0 = (h % 2) * 64
            return SF[p0:p0 + 64, ping * 512 + (h // 2) * 64: ping * 512 + (h // 2) * 64 + 64]

        def sbh(ping, h):
            p0 = (h % 2) * 64
            return SB[p0:p0 + 64, ping * 512 + (h // 2) * 64: ping * 512 + (h // 2) * 64 + 64]

        def xrs(sl):
            return XRS[:, sl * 512: sl * 512 + 512]

        def outslot(idx):
            sl = idx % 5
            if sl < 3:
                return XRS[:, sl * 512: sl * 512 + 512]
            if sl == 3:
                return QTILF[:].bitcast(F32)
            return VTILF[:].bitcast(F32)

        def tmpa(hh):
            return TMPA[:, hh * 512: hh * 512 + 512]

        ST0 = TMPA[0:1, 0:512]
        ST1 = TMPA[0:1, 512:1024]

        for dst, src in (
            (QKB[:], qkb), (OUTB[:], outb), (S1C[:], s1c), (S2C[:], s2c),
            (S1HC[:], s1hc), (S2HC[:], s2hc), (S3C[:], s3c),
            (IDENT[:], ident), (EMAT[:], emat), (TRI[:], tri), (T1M[:], t1m),
            (L0125[:], l0125), (C128[:], c128),
            (ONESR[:], onesr), (ONESBF[:], onesbf), (ONESCB[:], onescolb),
            (EPSC[:], epsc), (RCPD[:], rcpd), (NEGH[:], negh), (GAMR[:], gam), (CVR[:], cvr),
        ):
            P.emit("q", lambda h, d=dst, s=src: h.dma_start(out=d, in_=s), 16, sem="qi")
        q_small = P.cnt["qi"]

        P.emit("g", lambda h: h.dma_start(out=VBr[:], in_=vb), 16, sem="gi")
        P.emit("g", lambda h: h.dma_start(
            out=AAa[:, 0:16384].rearrange("p (c t) -> p c t", c=8),
            in_=xT.rearrange("(c p) t -> p c t", p=128)), 16, sem="gi")
        for i in range(2):
            P.emit("g", lambda h, i=i: h.dma_start(
                out=ABa[:, 24576 + i * 4096: 24576 + (i + 1) * 4096], in_=outw[i]), 16, sem="gi")
        g_init = P.cnt["gi"]

        P.wait("s", "qi", q_small)
        P.emit("s", lambda h: h.activation(GROW[:], GAMR[:], AF.Sigmoid), 1)
        P.wait_self_dma("s", "s", P.cnt["s"])
        P.emit("s", lambda h: h.activation(LOGG[:], GROW[:], AF.Ln), 1)
        s_lg = P.cnt["s"]
        P.wait("v", "s", s_lg)
        P.wait("v", "qi", q_small)
        for h_ in range(H):
            P.emit("v", lambda h, hh=h_: h.tensor_scalar(
                ABf[:, 4096 + hh * 128: 4096 + hh * 128 + 128], EMAT[:],
                LOGG[:, hh:hh + 1], None, ALU.mult), 1 if h_ == H - 1 else 0)
        v_mscr = P.cnt["v"]
        P.wait("s", "v", v_mscr)
        P.emit("s", lambda h: h.activation(ABf[:, 6144:8192], ABf[:, 4096:6144], AF.Exp), 1)
        s_mexp = P.cnt["s"]
        P.wait("v", "s", s_mexp)
        for h_ in range(H):
            P.emit("v", lambda h, hh=h_: h.tensor_tensor(
                mmask(hh), ABf[:, 6144 + hh * 128: 6144 + hh * 128 + 128], TRI[:], ALU.mult))
        for cq in range(8):
            for half in range(2):
                hh_ = 2 * cq + half
                P.emit("v", lambda h, cq=cq, half=half, hh_=hh_: h.tensor_scalar(
                    ABf[half * 64:half * 64 + 64, 4096 + cq * 128: 4096 + cq * 128 + 128],
                    T1M[half * 64:half * 64 + 64, :],
                    LOGG[half * 64:half * 64 + 64, hh_:hh_ + 1], None, ALU.mult))
        for h_ in range(H):
            P.emit("v", lambda h, hh=h_: h.tensor_scalar(
                ABf[:, 5120 + hh * 64: 5120 + hh * 64 + 64], CVR[:],
                LOGG[:, hh:hh + 1], None, ALU.mult), 1 if h_ == H - 1 else 0)
        v_dscr = P.cnt["v"]
        P.wait("s", "v", v_dscr)
        P.emit("s", lambda h: h.activation(ABf[:, 2048:3072], ABf[:, 4096:5120], AF.Exp))
        P.emit("s", lambda h: h.activation(ABf[:, 3072:4096], ABf[:, 5120:6144], AF.Exp, bias=L0125[:]), 1)
        P.emit("v", lambda h: h.tensor_scalar(GS1[:], LOGG[:], C128[:], None, ALU.mult), 1)
        v_gs2 = P.cnt["v"]
        P.wait("s", "v", v_gs2)
        P.emit("s", lambda h: h.activation(GCC[:], GS1[:], AF.Exp), 1)
        P.emit("v", lambda h: h.memset(SF[:], 0.0))
        P.emit("v", lambda h: h.memset(SB[:], 0.0), 1)
        s_init = P.cnt["s"]
        v_init = P.cnt["v"]

        if debug:
            P.wait("q", "s", s_init)
            P.wait("q", "v", v_init)
            P.emit("q", lambda h: h.dma_start(out=taps["tapMD"], in_=ABf[:, 0:4096]), 16, sem="qi")
            P.emit("q", lambda h: h.dma_start(out=taps["tapWG"][:, 16:32], in_=GCC[:]), 16, sem="qi")

        gslot = [0]
        pb_free_t = {}
        ret_info, w_of = [], []
        mm1_cnt, v1_cnt, back_cnt, s1_cnt, v4_cnt = {}, {}, {}, {}, {}
        s_pk, sb_cast, prep_cnt = {}, {}, {}
        qk_evac = {}
        v_evac = {}
        ret_v_done = {}
        ret_s_done = {}
        op_t = {}
        opv_last = {}
        ln1_v = {}
        x1_spill_q = {-1: 0}

        def gdma(fn):
            key = gslot[0]
            if key >= 2:
                P.wait("g", "t", pb_free_t.get(key - 2, 0))
            sl = key % 2
            gslot[0] += 1
            cnt = P.emit("g", lambda h, sl=sl, fn=fn: fn(h, sl), 16, sem="g%d" % sl)
            return key, sl, cnt

        def layer_norm(P, reg_in, reg_out, reg_scr, stats_src, n_sl, t_prewait=None, bulk=None, scr2=None, banks=None, strows=None, prev=None):
            rb, sqb = stats_src
            bk0, bk1 = banks if banks is not None else (PS[2][:], PS[3][:])
            st0, st1 = strows if strows is not None else (ST0, ST1)
            prev_tstats, prev_varf, prev_tbc, prev_apply = prev if prev is not None else (0, 0, 0, 0)
            P.wait("s", "t", prev_tstats)
            if bulk is not None:
                bin_, brb, bsqb = bulk
                P.emit("s", lambda h: h.copy(brb, bin_))
                P.emit("s", lambda h: h.activation(bsqb, bin_, AF.Square), 1)
            else:
                for oc in range(8):
                    P.emit("s", lambda h, oc=oc: h.copy(rb(oc), reg_in(oc)))
                    P.emit("s", lambda h, oc=oc: h.activation(sqb(oc), reg_in(oc), AF.Square),
                           1 if oc == 7 else 0)
            s_stat = P.cnt["s"]
            P.wait("t", "s", s_stat)
            P.wait("t", "v", prev_varf)
            for tv in t_prewait or []:
                P.wait("t", tv[0], tv[1])
            for oc in range(8):
                P.emit("t", lambda h, oc=oc, bk0=bk0: h.matmul(bk0[0:1, 0:512], ONESCB[:], rb(oc),
                                                      start=(oc == 0), stop=(oc == 7)), 1)
            for oc in range(8):
                P.emit("t", lambda h, oc=oc, bk1=bk1: h.matmul(bk1[0:1, 0:512], ONESCB[:], sqb(oc),
                                                      start=(oc == 0), stop=(oc == 7)), 1)
            t_stats = P.cnt["t"]
            P.wait("v", "t", t_stats)
            P.wait("v", "t", prev_tbc)
            P.wait_self_dma("v", "v", prev_apply)
            P.emit("v", lambda h, bk0=bk0, st0=st0: h.tensor_scalar(st0, bk0[0:1, 0:512], RCPD[:], None, ALU.mult))
            P.emit("v", lambda h, bk1=bk1, st1=st1: h.tensor_scalar(st1, bk1[0:1, 0:512], RCPD[:], None, ALU.mult), 1)
            P.wait_self_dma("v", "v", P.cnt["v"])
            P.emit("v", lambda h, st0=st0: h.tensor_copy(STB[0:1, 0:512], st0))
            P.emit("v", lambda h, st1=st1: h.tensor_copy(STB[0:1, 1024:1536], st1), 1)
            P.wait_self_dma("v", "v", P.cnt["v"])
            P.emit("v", lambda h, st0=st0: h.tensor_sub(XRS[0:1, 0:512], st0, STB[0:1, 0:512]))
            P.emit("v", lambda h, st1=st1: h.tensor_sub(XRS[0:1, 512:1024], st1, STB[0:1, 1024:1536]), 1)
            P.wait_self_dma("v", "v", P.cnt["v"])
            P.emit("v", lambda h: h.tensor_copy(STB[0:1, 512:1024], XRS[0:1, 0:512]))
            P.emit("v", lambda h: h.tensor_copy(STB[0:1, 1536:2048], XRS[0:1, 512:1024]), 1)
            v_st = P.cnt["v"]
            P.wait("t", "v", v_st)
            P.emit("t", lambda h, bk0=bk0: h.matmul(bk0, ONESBF[:], STB[0:1, 0:512], start=True, stop=False))
            P.emit("t", lambda h, bk0=bk0: h.matmul(bk0, ONESBF[:], STB[0:1, 512:1024], start=False, stop=True), 1)
            P.emit("t", lambda h, bk1=bk1: h.matmul(bk1, ONESBF[:], STB[0:1, 1024:1536], start=True, stop=False))
            P.emit("t", lambda h, bk1=bk1: h.matmul(bk1, ONESBF[:], STB[0:1, 1536:2048], start=False, stop=True), 1)
            t_bc = P.cnt["t"]
            mb, msq = scr2 if scr2 is not None else (tmpa(0), tmpa(1))
            P.wait("v", "t", t_bc)
            P.emit("v", lambda h, mb=mb, bk0=bk0: h.tensor_copy(mb, bk0), 1)
            P.wait_self_dma("v", "v", P.cnt["v"])
            P.emit("v", lambda h, mb=mb, msq=msq: h.tensor_mul(msq, mb, mb), 1)
            P.wait_self_dma("v", "v", P.cnt["v"])
            P.emit("v", lambda h, msq=msq, bk1=bk1: h.tensor_sub(xrs(0), bk1, msq), 1)
            v_varf = P.cnt["v"]
            P.wait("s", "v", v_varf)
            P.wait("s", "v", prev_apply)
            P.emit("s", lambda h: h.activation(xrs(1), xrs(0), AF.Ln, bias=EPSC[:]), 1)
            P.wait_self_dma("s", "s", P.cnt["s"])
            P.emit("s", lambda h: h.activation(xrs(2), xrs(1), AF.Exp, scale=NEGH[:]), 1)
            s_sqrt = P.cnt["s"]
            P.wait("v", "s", s_sqrt)
            mbrs = SF[:, 0:512].bitcast(F32).bitcast(F32) if False else SF[:, 0:512]
            P.emit("v", lambda h, mb=mb, mbrs=mbrs: h.tensor_mul(mbrs, mb, xrs(2)), 1)
            t1 = [SF[:, 512:1024], scr2[1], scr2[0]]
            P.wait_self_dma("v", "v", P.cnt["v"])
            mval, sval = {}, {}
            for oc in range(8):
                if oc >= 3:
                    P.wait_self_dma("v", "v", sval[oc - 3])
                mval[oc] = P.emit("v", lambda h, oc=oc, t1=t1: h.tensor_mul(
                    t1[oc % 3], reg_in(oc), xrs(2)), 1)
                if oc >= 1:
                    P.wait_self_dma("v", "v", mval[oc - 1])
                    sval[oc - 1] = P.emit("v", lambda h, oc=oc, t1=t1, mbrs=mbrs: h.tensor_sub(
                        reg_out(oc - 1), t1[(oc - 1) % 3], mbrs), 1)
            P.wait_self_dma("v", "v", mval[7])
            P.emit("v", lambda h, t1=t1, mbrs=mbrs: h.tensor_sub(reg_out(7), t1[7 % 3], mbrs), 1)
            svl = [sval[i] for i in range(7)] + [P.cnt["v"]]
            return P.cnt["v"], v_varf, svl

        ln1_varf = {-1: 0}
        ln1_cast_s = {-1: 0}
        ln1_sub_v = {-1: 0}
        ln1_tmp = {}

        def ln1_stage_a(w):
            P.wait("s", "v", opv_last[w])
            P.emit("s", lambda h: h.copy(ABa[:, 16384:20480], ABf[:, 4096:8192]))
            P.emit("s", lambda h: h.activation(ABa[:, 20480:24576], ABf[:, 4096:8192], AF.Square), 1)
            ln1_tmp[("s_stat", w)] = P.cnt["s"]

        def ln1_stage_b(w):
            bk0, bk1 = PS[2][:], PS[3][:]
            st0, st1 = ST0, ST1
            prev_tbc = 0
            P.wait("t", "s", ln1_tmp[("s_stat", w)])
            P.wait("t", "s", v_evac.get((w + 1, 7), v_evac[(w, 7)]))
            for oc in range(8):
                P.emit("t", lambda h, oc=oc: h.matmul(PS[2][0:1, 0:512], ONESCB[:], rbw(oc),
                                                      start=(oc == 0), stop=(oc == 7)), 1)
            for oc in range(8):
                P.emit("t", lambda h, oc=oc: h.matmul(PS[3][0:1, 0:512], ONESCB[:], sqbw(oc),
                                                      start=(oc == 0), stop=(oc == 7)), 1)
            t_stats = P.cnt["t"]
            P.wait("v", "t", t_stats)
            P.wait("v", "t", prev_tbc)
            P.emit("v", lambda h, bk0=bk0, st0=st0: h.tensor_scalar(st0, bk0[0:1, 0:512], RCPD[:], None, ALU.mult))
            P.emit("v", lambda h, bk1=bk1, st1=st1: h.tensor_scalar(st1, bk1[0:1, 0:512], RCPD[:], None, ALU.mult), 1)
            P.wait_self_dma("v", "v", P.cnt["v"])
            P.emit("v", lambda h, st0=st0: h.tensor_copy(STB[0:1, 0:512], st0))
            P.emit("v", lambda h, st1=st1: h.tensor_copy(STB[0:1, 1024:1536], st1), 1)
            P.wait_self_dma("v", "v", P.cnt["v"])
            P.emit("v", lambda h, st0=st0: h.tensor_sub(XRS[0:1, 0:512], st0, STB[0:1, 0:512]))
            P.emit("v", lambda h, st1=st1: h.tensor_sub(XRS[0:1, 512:1024], st1, STB[0:1, 1024:1536]), 1)
            P.wait_self_dma("v", "v", P.cnt["v"])
            P.emit("v", lambda h: h.tensor_copy(STB[0:1, 512:1024], XRS[0:1, 0:512]))
            P.emit("v", lambda h: h.tensor_copy(STB[0:1, 1536:2048], XRS[0:1, 512:1024]), 1)
            v_st = P.cnt["v"]
            P.wait("t", "v", v_st)
            P.emit("t", lambda h, bk0=bk0: h.matmul(bk0, ONESBF[:], STB[0:1, 0:512], start=True, stop=False))
            P.emit("t", lambda h, bk0=bk0: h.matmul(bk0, ONESBF[:], STB[0:1, 512:1024], start=False, stop=True), 1)
            P.emit("t", lambda h, bk1=bk1: h.matmul(bk1, ONESBF[:], STB[0:1, 1024:1536], start=True, stop=False))
            P.emit("t", lambda h, bk1=bk1: h.matmul(bk1, ONESBF[:], STB[0:1, 1536:2048], start=False, stop=True), 1)
            ln1_tmp[("t_bc", w)] = P.cnt["t"]

        def ln1_stage_c1(w):
            bk0, bk1 = PS[2][:], PS[3][:]
            P.wait("v", "t", ln1_tmp[("t_bc", w)])
            P.emit("v", lambda h: h.tensor_copy(tmpa(0), PS[2][:]), 1)
            P.wait_self_dma("v", "v", P.cnt["v"])
            P.emit("v", lambda h: h.tensor_mul(tmpa(1), tmpa(0), tmpa(0)), 1)
            P.wait_self_dma("v", "v", P.cnt["v"])
            P.emit("v", lambda h: h.tensor_sub(xrs(0), PS[3][:], tmpa(1)), 1)
            v_vf = P.cnt["v"]
            ln1_varf[w] = v_vf
            P.wait("s", "v", v_vf)
            P.emit("s", lambda h: h.activation(xrs(1), xrs(0), AF.Ln, bias=EPSC[:]), 1)
            P.wait_self_dma("s", "s", P.cnt["s"])
            P.emit("s", lambda h: h.activation(xrs(2), xrs(1), AF.Exp, scale=NEGH[:]), 1)
            ln1_tmp[("s_rs", w)] = P.cnt["s"]

        def ln1_stage_c2(w):
            P.wait("v", "s", ln1_tmp[("s_rs", w)])
            for oc in range(8):
                P.emit("v", lambda h, oc=oc: h.tensor_sub(xfw(oc), r1w(oc), tmpa(0)),
                       1 if oc == 7 else 0)
            ln1_sub_v[w] = P.cnt["v"]

        def ln1_stage_c3(w):
            t0 = w * TW
            P.wait_self_dma("v", "v", ln1_sub_v[w])
            for oc in range(8):
                P.emit("v", lambda h, oc=oc: h.tensor_mul(r1w(oc), xfw(oc), xrs(2)),
                       1 if oc == 7 else 0)
            v_x1 = P.cnt["v"]
            ln1_v[w] = v_x1
            P.wait("s", "v", v_x1)
            for oc in range(8):
                P.emit("s", lambda h, oc=oc, t0=t0: h.copy(x1t(oc, t0, TW), r1w(oc)),
                       1 if oc == 7 else 0)
            ln1_cast_s[w] = P.cnt["s"]

        def ln1_stage_c(w):
            ln1_stage_c1(w)
            ln1_stage_c2(w)
            ln1_stage_c3(w)

        for w in range(NW):
            t0 = w * TW
            wkeys = {}

            def wdma(idx, wkeys=None):
                if idx < 4:
                    fn = lambda h, sl, gi=idx: h.dma_start(out=PB[sl][:], in_=wqk[gi])
                else:
                    fn = lambda h, sl, vi=idx - 4: h.dma_start(out=PB[sl][:], in_=wvr[vi])
                wkeys[idx] = gdma(fn)

            wdma(0, wkeys)
            wdma(1, wkeys)
            if w > 0:
                ln1_stage_a(w - 1)

            prev_ret_v = ret_v_done.get(w - 1, 0)
            prev_op_t = op_t.get(w - 1, 0)
            prev_opv = opv_last.get(w - 1, 0)

            for ot in range(16):
                gi = ot // 4
                key, sl, gcnt = wkeys[gi]
                bank = ot % 4
                P.wait("t", "g%d" % sl, gcnt)
                if w == 0 and ot == 0:
                    P.wait("t", "v", v_init)
                    P.wait("t", "s", s_init)
                    P.wait("t", "gi", g_init)
                if ot < 4:
                    P.wait("t", "v", prev_opv)
                else:
                    P.wait("t", "v", qk_evac[(w, ot - 4)])
                for c in range(8):
                    P.emit("t", lambda h, ot=ot, c=c, sl=sl, bank=bank, t0=t0:
                           h.matmul(PS[bank][:],
                                    PB[sl][:, (ot % 4) * 1024 + c * 128:(ot % 4) * 1024 + c * 128 + 128],
                                    xtb(c, t0, TW), start=(c == 0), stop=(c == 7)), 1)
                pb_free_t[key] = P.cnt["t"]
                t_tile = P.cnt["t"]
                P.wait("v", "t", t_tile)
                if w > 0:
                    P.wait("v", "t", prev_op_t)
                dst = qtw(ot % 8) if ot < 8 else ktw(ot % 8)
                P.emit("v", lambda h, dst=dst, bank=bank, ot=ot:
                       h.tensor_scalar(dst, PS[bank][:], QKB[:, ot:ot + 1], None, ALU.add), 1)
                qk_evac[(w, ot)] = P.cnt["v"]
                if ot % 4 == 3 and ot // 4 + 2 <= 5:
                    wdma(ot // 4 + 2, wkeys)

            for tt in range(4):
                for ih in range(2):
                    bank = (tt * 2 + ih) % 4
                    P.wait("t", "v", qk_evac[(w, 12 + bank)])
                    if tt >= 2:
                        P.wait("t", "s", v_evac[(w, (tt - 2) * 2 + ih)])
                    for cc in range(8):
                        gk = wkeys[4 + cc // 4]
                        P.wait("t", "g%d" % gk[1], gk[2])
                        P.emit("t", lambda h, tt=tt, ih=ih, cc=cc, bank=bank, sl=gk[1], t0=t0:
                               h.matmul(PS[bank][:], xtb(cc, t0 + tt * 128, 128),
                                        PB[sl][:, (cc % 4) * 1024 + ih * 512:(cc % 4) * 1024 + ih * 512 + 512],
                                        start=(cc == 0), stop=False), 1)
                    P.wait("t", "gi", g_init)
                    P.emit("t", lambda h, ih=ih, bank=bank:
                           h.matmul(PS[bank][:], ONESBF[:], VBr[0:1, ih * 512: ih * 512 + 512],
                                    start=False, stop=True), 1)
                    t_vt = P.cnt["t"]
                    P.wait("s", "t", t_vt)
                    if w > 0:
                        P.wait("s", "t", prev_op_t)
                    P.emit("s", lambda h, tt=tt, ih=ih, bank=bank:
                           h.copy(vslice(tt, ih * 512, 512), PS[bank][:]), 1)
                    v_evac[(w, tt * 2 + ih)] = P.cnt["s"]
            pb_free_t[wkeys[4][0]] = P.cnt["t"]
            pb_free_t[wkeys[5][0]] = P.cnt["t"]

            if w > 0:
                ln1_stage_b(w - 1)

            if debug and w == 0:
                P.wait("q", "v", P.cnt["v"])
                P.emit("q", lambda h: h.dma_start(out=taps["tapQT"], in_=AAa[:, 16384:20480]), 16, sem="qi")
                P.emit("q", lambda h: h.dma_start(out=taps["tapKT"], in_=AAa[:, 20480:24576]), 16, sem="qi")
                P.emit("q", lambda h: h.dma_start(out=taps["tapV"], in_=AAa[:, 24576:28672]), 16, sem="qi")
                P.wait("v", "qi", P.cnt["qi"])
                P.wait("s", "qi", P.cnt["qi"])

            def emit_back(j):
                cl_j, h_j = ret_info[j]
                cglob_j = w_of[j] * 4 + cl_j
                par = j % 2
                p0 = (h_j % 2) * 64
                ping = cglob_j % 2
                prev = 1 - ping
                cqpar = (h_j // 2) % 2
                P.wait("t", "v", v1_cnt[j])
                P.wait("t", "v", prep_cnt[cglob_j])
                P.wait("t", "s", s_pk[(cglob_j * 8 + h_j // 2)])
                P.wait("t", "s", v_evac[(w_of[j], cl_j * 2 + h_j // 8)])
                if cglob_j > 0:
                    P.wait("t", "v", sb_cast[(cglob_j - 1, h_j // 8)])
                if j >= 2:
                    P.wait("t", "s", s1_cnt[j - 2])
                    P.wait("t", "v", v4_cnt[j - 2])
                P.emit("t", lambda h, h_j=h_j, cl_j=cl_j, par=par, prev=prev, p0=p0:
                       h.matmul(PS[par][p0:p0 + 64, 0:128], sbh(prev, h_j), qtilf(h_j, cl_j),
                                start=True, stop=False, tile_position=(p0, p0)), 1)
                P.emit("t", lambda h, h_j=h_j, cl_j=cl_j, par=par, p0=p0:
                       h.matmul(PS[par][p0:p0 + 64, 0:128], vslice(cl_j, h_j * 64, 64), atil(par),
                                start=False, stop=True, tile_position=(0, p0)), 1)
                P.emit("t", lambda h, h_j=h_j, cl_j=cl_j, par=par, p0=p0, cqpar=cqpar:
                       h.matmul(PS[par][p0:p0 + 64, 128:192], ktok2(cqpar, h_j), vtilf(h_j, cl_j),
                                start=True, stop=True, tile_position=(0, p0)), 1)
                back_cnt[j] = P.cnt["t"]
                P.wait("s", "t", back_cnt[j])
                P.emit("s", lambda h, h_j=h_j, cl_j=cl_j, par=par, p0=p0:
                       h.copy(yhead(h_j, cl_j), PS[par][p0:p0 + 64, 0:128]), 1)
                s1_cnt[j] = P.cnt["s"]
                P.wait("v", "t", back_cnt[j])
                P.wait("v", "s", s1_cnt[j])
                P.emit("v", lambda h, h_j=h_j, par=par, ping=ping, prev=prev, p0=p0:
                       h.scalar_tensor_tensor(sfh(ping, h_j), sfh(prev, h_j),
                                              GCC[p0:p0 + 64, h_j:h_j + 1],
                                              PS[par][p0:p0 + 64, 128:192],
                                              ALU.mult, ALU.add), 1)
                v4_cnt[j] = P.cnt["v"]
                if h_j in (7, 15):
                    half = h_j // 8
                    P.wait_self_dma("v", "v", P.cnt["v"])
                    P.emit("v", lambda h, ping=ping, half=half: h.tensor_copy(
                        SB[:, ping * 512 + half * 256: ping * 512 + half * 256 + 256],
                        SF[:, ping * 512 + half * 256: ping * 512 + half * 256 + 256]), 1)
                    sb_cast[(cglob_j, half)] = P.cnt["v"]

            for cl in range(4):
                cglob = w * 4 + cl
                for hh in range(H):
                    if cl == 3 and w > 0:
                        if hh == 0:
                            ln1_stage_c1(w - 1)
                        elif hh == 6:
                            ln1_stage_c2(w - 1)
                        elif hh == 12:
                            ln1_stage_c3(w - 1)
                    if hh == (1 if cglob > 0 else 0):
                        if cglob >= 1 and (cglob - 1) * 16 + 15 in back_cnt:
                            P.wait("v", "t", back_cnt[(cglob - 1) * 16 + 15])
                        P.wait("v", "s", v_evac[(w, cl * 2 + 1)])
                        P.emit("v", lambda h, cl=cl: h.tensor_tensor(
                            QTILF[:].rearrange("p (c t) -> p c t", c=8),
                            AAa[:, 16384:20480].rearrange("p (c t) -> p c t", c=8)[:, :, cl * 128: cl * 128 + 128],
                            dfull3d(), ALU.mult))
                        P.emit("v", lambda h, cl=cl: h.tensor_tensor(
                            VTILF[:],
                            AAa[:, 24576 + cl * 1024: 24576 + cl * 1024 + 1024],
                            wupfull(), ALU.mult), 1)
                        prep_cnt[cglob] = P.cnt["v"]
                    u = len(ret_info)
                    ret_info.append((cl, hh))
                    w_of.append(w)
                    par = u % 2
                    p0 = (hh % 2) * 64
                    P.wait("t", "v", qk_evac[(w, hh // 2)])
                    P.wait("t", "v", qk_evac[(w, 8 + hh // 2)])
                    if u >= 2:
                        P.wait("t", "v", v1_cnt[u - 2])
                    if hh % 2 == 0:
                        gpair = cglob * 8 + hh // 2
                        cqpar = (hh // 2) % 2
                        if gpair >= 2:
                            P.wait("t", "s", s_pk[gpair - 2])
                        P.emit("t", lambda h, hh=hh, cl=cl, cqpar=cqpar:
                               h.transpose(PK[cqpar][:, 0:128],
                                           AAa[:, 20480 + (hh // 2) * 512 + cl * 128:
                                               20480 + (hh // 2) * 512 + cl * 128 + 128],
                                           IDENT[:]), 1)
                        t_trans = P.cnt["t"]
                        P.wait("s", "t", t_trans)
                        if gpair >= 2 and (gpair - 2) * 2 + 1 in back_cnt:
                            P.wait("s", "t", back_cnt[(gpair - 2) * 2 + 1])
                        P.emit("s", lambda h, cqpar=cqpar:
                               h.copy(KTOK2[:, cqpar * 128: cqpar * 128 + 128],
                                      PK[cqpar][:, 0:128]), 1)
                        s_pk[gpair] = P.cnt["s"]
                    P.emit("t", lambda h, hh=hh, cl=cl, par=par, p0=p0:
                           h.matmul(PS[4 + par][:, 0:128], khead(hh, cl), qhead(hh, cl),
                                    start=True, stop=True, tile_position=(p0, 0)), 1)
                    mm1_cnt[u] = P.cnt["t"]
                    P.wait("v", "t", mm1_cnt[u])
                    P.emit("v", lambda h, hh=hh, par=par:
                           h.tensor_tensor(atil(par), PS[4 + par][:, 0:128], mmask(hh), ALU.mult), 1)
                    v1_cnt[u] = P.cnt["v"]
                    if u >= 1 and back_cnt.get(u - 1) is None:
                        emit_back(u - 1)
            emit_back(len(ret_info) - 1)
            ret_v_done[w] = P.cnt["v"]
            ret_s_done[w] = P.cnt["s"]

            if debug and w == 0:
                P.wait("q", "s", ret_s_done[0])
                P.wait("q", "v", ret_v_done[0])
                P.emit("q", lambda h: h.dma_start(out=taps["tapYT"], in_=AAa[:, 28672:32768]), 16, sem="qi")
                P.emit("q", lambda h: h.dma_start(out=taps["tapSB"], in_=SB[:]), 16, sem="qi")
                P.wait("s", "qi", P.cnt["qi"])
                P.wait("v", "qi", P.cnt["qi"])

            P.wait("t", "gi", g_init)
            opevac = {}
            for oc in range(8):
                bank = oc % 4
                if oc >= 3:
                    P.wait("q", "v", opevac[oc - 3])
                else:
                    P.wait("q", "v", ln1_v.get(w - 1, 0))
                P.emit("q", lambda h, oc=oc, t0=t0: h.dma_start(
                    out=xrs(oc % 3), in_=xT[oc * 128:(oc + 1) * 128, t0:t0 + TW]), 16,
                    sem="q%d" % (oc % 3))
                q_rr = ("q%d" % (oc % 3), P.cnt["q%d" % (oc % 3)])
                P.wait("t", "s", ret_s_done[w])
                P.wait("t", "v", ret_v_done[w])
                P.wait("t", "v", ln1_varf.get(w - 1, 0))
                if oc >= 4:
                    P.wait("t", "v", opevac[oc - 4])
                for c in range(8):
                    P.emit("t", lambda h, oc=oc, c=c, bank=bank:
                           h.matmul(PS[bank][:], outwb(oc, c * 128, 128), ytw(c),
                                    start=(c == 0), stop=(c == 7)), 1)
                t_tile = P.cnt["t"]
                P.wait("v", "t", t_tile)
                P.wait("v", q_rr[0], q_rr[1])
                P.wait("v", "s", ln1_cast_s.get(w - 1, 0))
                P.emit("v", lambda h, oc=oc, bank=bank:
                       h.scalar_tensor_tensor(r1w(oc), PS[bank][:], OUTB[:, oc:oc + 1],
                                              xrs(oc % 3), ALU.add, ALU.add), 1)
                opevac[oc] = P.cnt["v"]
            op_t[w] = P.cnt["t"]
            opv_last[w] = opevac[7]
            if debug and w == 0:
                P.wait("q", "v", opevac[7])
                P.emit("q", lambda h: h.dma_start(out=taps["tapR1"], in_=ABf[:, 4096:8192]), 16, sem="qi")
                P.wait("v", "qi", P.cnt["qi"])
                P.wait("s", "qi", P.cnt["qi"])
            if w == NW - 1:
                ln1_stage_a(w)
                ln1_stage_b(w)
                ln1_stage_c(w)

        if debug:
            P.wait("q", "s", P.cnt["s"])
            P.emit("q", lambda h: h.dma_start(out=taps["tapX1"], in_=X1T[:]), 16, sem="qi")

        att_s = P.cnt["s"]
        ffn_state = {"store_q": 0, "ln2_v": 0, "out_v": {}}
        store_q = 0
        ln2_v = 0
        prev_out_v = 0
        PKF = [PK[i][:].bitcast(F32) for i in range(2)]

        store_guard = {0: 0, 1: 0}

        def do_ln2(tb, ths=(0, 1)):
            nonlocal store_q, ln2_v
            out_v_tb = ffn_state["out_v"][tb]
            prev_vout = ffn_state.get("ln2_pv", 0)
            prev_tstats = ffn_state.get("ln2_pt", 0)
            prev_varf = ffn_state.get("ln2_pvf", 0)
            yTr = yT.rearrange("(c p) t -> p c t", p=128)
            for th in ths:
                P.wait("s", "v", out_v_tb)
                if tb == 1 and th == 0:
                    P.wait("v", "qs", P.cnt["qs"])
                if th == 0:
                    ro = outf
                    src = (rb2, sqb2)
                    blk = (AAf[:, 0:8192].rearrange("p (c t) -> p c t", c=8)[:, :, 0:512],
                           AAa[:, 24576:28672].rearrange("p (c t) -> p c t", c=8),
                           AAa[:, 28672:32768].rearrange("p (c t) -> p c t", c=8))
                    sin = AAf[:, 8192:12288].rearrange("p (c t) -> p c t", c=8)
                else:
                    ro = lambda oc: r2b(oc, 0, 512)
                    src = ((lambda oc, tb=tb: X1T[:, oc * 2048 + tb * 1024: oc * 2048 + tb * 1024 + 512]),
                           (lambda oc, tb=tb: X1T[:, oc * 2048 + tb * 1024 + 512: oc * 2048 + tb * 1024 + 1024]))
                    blk = (AAf[:, 0:8192].rearrange("p (c t) -> p c t", c=8)[:, :, 512:1024],
                           X1T[:].rearrange("p (c t) -> p c t", c=8)[:, :, tb * 1024: tb * 1024 + 512],
                           X1T[:].rearrange("p (c t) -> p c t", c=8)[:, :, tb * 1024 + 512: tb * 1024 + 1024])
                    sin = AAf[:, 0:8192].rearrange("p (c t) -> p c t", c=8)[:, :, 0:512]
                v_out, _vf2, svl = layer_norm(
                    P,
                    (lambda oc, th=th: r2b(oc, th * 512, 512)),
                    ro, xf2, src, 3,
                    bulk=blk,
                    scr2=(QTILF[:].bitcast(F32), VTILF[:].bitcast(F32)),
                    banks=(PKF[0][:, 0:512], PKF[1][:, 0:512]),
                    strows=(SF[0:1, 0:512], SF[0:1, 512:1024]),
                    prev=(prev_tstats, prev_varf, prev_tstats, prev_vout))
                prev_vout = v_out
                prev_varf = _vf2
                prev_tstats = P.cnt["t"]
                if tb == 1:
                    for oc in range(8):
                        P.wait("q", "v", svl[oc])
                        P.emit("q", lambda h, tb=tb, th=th, oc=oc, sin=sin, yTr=yTr: h.dma_start(
                            out=yTr[:, oc:oc + 1, tb * TB + th * 512: tb * TB + th * 512 + 512],
                            in_=sin[:, oc:oc + 1, 0:512]), 16, sem="qs")
                else:
                    P.wait("q", "v", v_out)
                    P.emit("q", lambda h, tb=tb, th=th, sin=sin, yTr=yTr: h.dma_start(
                        out=yTr[:, :, tb * TB + th * 512: tb * TB + th * 512 + 512],
                        in_=sin), 16, sem="qs")
                store_q = P.cnt["qs"]
                store_guard[th] = store_q
                ffn_state["ln2_pv"] = prev_vout
                ffn_state["ln2_pt"] = prev_tstats
                ffn_state["ln2_pvf"] = prev_varf
            ln2_v = prev_vout

        for tb in range(2):
            tg0 = tb * TB
            uevac_v = {}
            uevac_s = {}
            vt_cons = {}
            vt = 0
            for ot in range(32):
                key, sl, gcnt = gdma(lambda h, sl, ot=ot: h.dma_start(out=PB[sl][:, 0:2048], in_=w12[ot]))
                P.wait("t", "g%d" % sl, gcnt)
                if tb == 0 and ot == 0:
                    P.wait("t", "s", ln1_cast_s[1])
                    P.wait("t", "v", opv_last[NW - 1])
                    P.wait("t", "v", ret_v_done[NW - 1])
                    P.wait("t", "s", ret_s_done[NW - 1])
                if tb == 1 and ot == 0:
                    P.wait("t", "v", prev_out_v)
                if tb == 1 and ot == 2:
                    do_ln2(0, ths=(0, 1))
                for wi in range(2):
                    pair = vt % 3
                    if tb == 0 and pair == 1 and vt < 3:
                        P.wait("t", "v", ln1_varf[NW - 1])
                    if vt >= 3:
                        eng, cval = vt_cons[vt - 3]
                        P.wait("t", eng, cval)
                    for c in range(8):
                        for th in range(2):
                            P.emit("t", lambda h, wi=wi, th=th, c=c, pair=pair, sl=sl, tg0=tg0:
                                   h.matmul(PS[pair * 2 + th][:],
                                            PB[sl][:, wi * 1024 + c * 128: wi * 1024 + c * 128 + 128],
                                            x1t(c, tg0 + th * 512, 512),
                                            start=(c == 0), stop=(c == 7)), 1)
                    t_u = P.cnt["t"]
                    if wi == 0:
                        P.wait("s", "t", t_u)
                        if ot >= 1:
                            P.wait("s", "v", uevac_v[ot - 1])
                        if tb == 0 and ot == 0:
                            P.wait("s", "v", ln1_sub_v[NW - 1])
                        for th in range(2):
                            P.emit("s", lambda h, ot=ot, th=th, pair=pair:
                                   h.activation(tmpa(th), PS[pair * 2 + th][:], AF.Silu,
                                                scale=S1C[:, ot:ot + 1]), 1)
                        uevac_s[ot] = P.cnt["s"]
                        vt_cons[vt] = ("s", uevac_s[ot])
                    else:
                        P.wait("v", "s", uevac_s[ot])
                        P.wait("v", "t", t_u)
                        if tb == 0 and ot == 8:
                            P.wait("v", "s", ln1_cast_s[NW - 1])
                        for th in range(2):
                            P.emit("v", lambda h, ot=ot, th=th, pair=pair:
                                   h.scalar_tensor_tensor(ub(ot, th * 512, 512), PS[pair * 2 + th][:],
                                                          S2C[:, ot:ot + 1], tmpa(th),
                                                          ALU.mult, ALU.mult), 1)
                        uevac_v[ot] = P.cnt["v"]
                        vt_cons[vt] = ("v", uevac_v[ot])
                    vt += 1
                pb_free_t[key] = P.cnt["t"]
            if debug and tb == 0:
                P.wait("q", "v", P.cnt["v"])
                P.emit("q", lambda h: h.dma_start(out=taps["tapU"], in_=ABa[:]), 16, sem="qi")
                P.wait("s", "qi", P.cnt["qi"])
                P.wait("v", "qi", P.cnt["qi"])
                P.wait("t", "qi", P.cnt["qi"])

            h1evac = {}
            for ot in range(32):
                key, sl, gcnt = gdma(lambda h, sl, ot=ot: h.dma_start(out=PB[sl][:], in_=h1p[ot]))
                P.wait("t", "g%d" % sl, gcnt)
                if ot == 0:
                    banks = (PKF[0][:, 0:512], PKF[1][:, 0:512])
                    P.wait("t", "v", ffn_state.get("ln2_pvf", 0))
                elif ot == 1:
                    banks = (PS[2][:], PS[3][:])
                    P.wait("t", "v", uevac_v[30])
                elif ot == 2:
                    banks = (PS[4][:], PS[5][:])
                    P.wait("t", "s", uevac_s[31])
                else:
                    banks = (PS[(ot % 3) * 2][:], PS[(ot % 3) * 2 + 1][:])
                    P.wait("t", "s", h1evac[ot - 3])
                for c in range(32):
                    P.wait("t", "v", uevac_v[c])
                    for th in range(2):
                        P.emit("t", lambda h, c=c, th=th, bk=banks[th], sl=sl:
                               h.matmul(bk, PB[sl][:, c * 128: c * 128 + 128],
                                        ub(c, th * 512, 512), start=(c == 0), stop=(c == 31)), 1)
                pb_free_t[key] = P.cnt["t"]
                t_h = P.cnt["t"]
                P.wait("s", "t", t_h)
                P.wait("s", "qs", store_q)
                for th in range(2):
                    P.emit("s", lambda h, ot=ot, th=th, bk=banks[th]:
                           h.activation(h1b(ot, th * 512, 512), bk, AF.Gelu,
                                        scale=S1HC[:, ot:ot + 1]), 1)
                h1evac[ot] = P.cnt["s"]
            if debug and tb == 0:
                P.wait("q", "s", P.cnt["s"])
                P.emit("q", lambda h: h.dma_start(out=taps["tapH1"], in_=AAa[:]), 16, sem="qi")
                P.wait("s", "qi", P.cnt["qi"])
                P.wait("v", "qi", P.cnt["qi"])
                P.wait("t", "qi", P.cnt["qi"])

            h2evac = {}
            for ot in range(32):
                key, sl, gcnt = gdma(lambda h, sl, ot=ot: h.dma_start(out=PB[sl][:], in_=h2p[ot]))
                P.wait("t", "g%d" % sl, gcnt)
                if ot == 0:
                    banks = (PKF[0][:, 0:512], PKF[1][:, 0:512])
                    P.wait("t", "s", h1evac[0])
                elif ot == 1:
                    banks = (PS[2][:], PS[3][:])
                    P.wait("t", "s", h1evac[31])
                elif ot == 2:
                    banks = (PS[4][:], PS[5][:])
                    P.wait("t", "s", h1evac[29])
                else:
                    banks = (PS[(ot % 3) * 2][:], PS[(ot % 3) * 2 + 1][:])
                    P.wait("t", "s", h2evac[ot - 3])
                    if ot == 3:
                        P.wait("t", "s", h1evac[30])
                for c in range(32):
                    P.wait("t", "s", h1evac[c])
                    for th in range(2):
                        P.emit("t", lambda h, c=c, th=th, bk=banks[th], sl=sl:
                               h.matmul(bk, PB[sl][:, c * 128: c * 128 + 128],
                                        h1b(c, th * 512, 512), start=(c == 0), stop=(c == 31)), 1)
                pb_free_t[key] = P.cnt["t"]
                t_h = P.cnt["t"]
                P.wait("s", "t", t_h)
                for th in range(2):
                    P.emit("s", lambda h, ot=ot, th=th, bk=banks[th]:
                           h.activation(h2b(ot, th * 512, 512), bk, AF.Gelu,
                                        scale=S2HC[:, ot:ot + 1]), 1)
                h2evac[ot] = P.cnt["s"]

            outevac = {}
            for ot in range(8):
                key, sl, gcnt = gdma(lambda h, sl, ot=ot: h.dma_start(out=PB[sl][:], in_=w3p[ot]))
                P.wait("t", "g%d" % sl, gcnt)
                if ot == 0:
                    banks = (PKF[0][:, 0:512], PKF[1][:, 0:512])
                    P.wait("t", "s", h2evac[0])
                elif ot == 1:
                    banks = (PS[2][:], PS[3][:])
                    P.wait("t", "s", h2evac[31])
                elif ot == 2:
                    banks = (PS[4][:], PS[5][:])
                    P.wait("t", "s", h2evac[29])
                else:
                    banks = (PS[(ot % 3) * 2][:], PS[(ot % 3) * 2 + 1][:])
                    P.wait("t", "v", outevac[ot - 3])
                    if ot == 3:
                        P.wait("t", "s", h2evac[30])
                for c in range(32):
                    P.wait("t", "s", h2evac[c])
                    for th in range(2):
                        P.emit("t", lambda h, c=c, th=th, bk=banks[th], sl=sl:
                               h.matmul(bk, PB[sl][:, c * 128: c * 128 + 128],
                                        h2b(c, th * 512, 512), start=(c == 0), stop=(c == 31)), 1)
                pb_free_t[key] = P.cnt["t"]
                t_o = P.cnt["t"]
                P.wait("v", "t", t_o)
                if ot == 0:
                    P.wait("v", "s", ln1_cast_s[NW - 1])
                for th in range(2):
                    P.emit("v", lambda h, ot=ot, th=th, bk=banks[th], tg0=tg0:
                           h.scalar_tensor_tensor(r2b(ot, th * 512, 512), bk,
                                                  S3C[:, ot:ot + 1], x1t(ot, tg0 + th * 512, 512),
                                                  ALU.mult, ALU.add), 1)
                outevac[ot] = P.cnt["v"]
            out_v = P.cnt["v"]
            prev_out_v = out_v
            ffn_state["out_v"][tb] = out_v

            if tb == 1:
                do_ln2(1, ths=(0, 1))

        P.wait_self_dma("q", "qs", P.cnt["qs"])
        P.emit("q", lambda h: h.nop())

        block = e(nc.Block())

        @block.tensor
        def _(t):
            P.run("t", t, sems)

        @block.vector
        def _(v):
            P.run("v", v, sems)

        @block.scalar
        def _(s):
            P.run("s", s, sems)

        @block.gpsimd
        def _(g):
            P.run("g", g, sems)

        @block.sync
        def _(q):
            P.run("q", q, sems)

    return nc


def _panels(w):
    O, I = w.shape
    no, ni = O // 128, I // 128
    return np.ascontiguousarray(w.reshape(no, 128, ni, 128).transpose(0, 3, 2, 1)).reshape(no, 128, I)


def _colvec(v):
    return np.ascontiguousarray(v.reshape(-1, 128).T)


def _group4(panels):
    n = panels.shape[0]
    return np.ascontiguousarray(panels.reshape(n // 4, 4, 128, 1024).transpose(0, 2, 1, 3)).reshape(n // 4, 128, 4096)


def kernel(**inputs):
    import ml_dtypes

    x = inputs["L0_attn_x"]
    wqkv = inputs["L0_attn_Wqkv_w"]
    bqkv = inputs["L0_attn_Wqkv_b"]

    qk_panels = _panels(wqkv[0:2048])
    wv = wqkv[2048:3072]
    wvr_c = np.ascontiguousarray(wv.reshape(1024, 8, 128).transpose(1, 2, 0))
    fin = dict(
        wqk=_group4(qk_panels),
        wvr=_group4(wvr_c),
        vb=np.ascontiguousarray(bqkv[2048:3072]).reshape(1, 1024),
        qkb=_colvec(bqkv[0:2048]),
        outw=_group4(_panels(inputs["L0_attn_out_w"])),
        outb=_colvec(inputs["L0_attn_out_b"]),
        w12=np.ascontiguousarray(
            np.concatenate([_panels(inputs["L0_ffn_w1_w"]), _panels(inputs["L0_ffn_w2_w"])], axis=2)),
        s1c=_colvec(inputs["L0_ffn_w1_s"]), s2c=_colvec(inputs["L0_ffn_w2_s"]),
        h1p=_panels(inputs["L0_ffn_h1_w"]), s1hc=_colvec(inputs["L0_ffn_h1_s"]),
        h2p=_panels(inputs["L0_ffn_h2_w"]), s2hc=_colvec(inputs["L0_ffn_h2_s"]),
        w3p=_panels(inputs["L0_ffn_w3_w"]), s3c=_colvec(inputs["L0_ffn_w3_s"]),
        gam=np.ascontiguousarray(np.broadcast_to(inputs["L0_attn_gamma_raw"].reshape(1, H), (128, H))),
        ident=np.eye(128, dtype=ml_dtypes.bfloat16),
        emat=np.maximum(np.arange(128)[None, :] - np.arange(128)[:, None], 0).astype(np.float32),
        tri=(0.125 * (np.arange(128)[:, None] <= np.arange(128)[None, :])).astype(np.float32),
        t1m=np.broadcast_to((np.arange(128) + 1).astype(np.float32), (128, 128)).copy(),
        l0125=np.full((128, 1), np.log(0.125), np.float32),
        c128=np.full((128, 1), 128.0, np.float32),
        onesr=np.ones((1, 128), np.float32),
        onesbf=np.ones((1, 128), ml_dtypes.bfloat16),
        onescolb=np.ones((128, 1), ml_dtypes.bfloat16),
        epsc=np.full((128, 1), 1e-5, np.float32),
        rcpd=np.full((1, 1), 1.0 / D, np.float32),
        negh=np.full((128, 1), -0.5, np.float32),
        cvr=np.broadcast_to((127.0 - np.arange(128)).astype(np.float32).reshape(128, 1), (128, 64)).copy(),
    )

    nc = build(debug=False)
    in_maps = []
    for b in range(NCORE):
        m = dict(fin)
        m["xT"] = np.ascontiguousarray(x[b].T)
        in_maps.append(m)
    res = run_bass_kernel_spmd(nc, in_maps, core_ids=list(range(NCORE)))
    out = np.stack([np.ascontiguousarray(res.results[b]["yT"].T) for b in range(NCORE)], axis=0)
    return out


# version 17
# speedup vs baseline: 1.0241x; 1.0208x over previous
import numpy as np
from contextlib import ExitStack

import concourse.bass as bass
import concourse.mybir as mybir
from concourse.bass_utils import run_bass_kernel_spmd

F32 = mybir.dt.float32
BF16 = mybir.dt.bfloat16
AF = mybir.ActivationFunctionType
ALU = mybir.AluOpType

D = 1024
T = 2048
H = 16
TW = 512
NW = T // TW
TB = 1024
NCORE = 8

ENGS = ("t", "v", "s", "g", "q")
DMASEMS = ("g0", "g1", "gi", "qi", "q0", "q1", "q2", "qs")
ALLSEMS = ENGS + DMASEMS


class Prog:
    def __init__(self):
        self.ops = {e: [] for e in ENGS}
        self.cnt = {e: 0 for e in ALLSEMS}
        self.lastw = {e: {f: 0 for f in ALLSEMS} for e in ENGS}

    def wait(self, eng, on, val):
        if val <= 0 or eng == on:
            return
        assert val <= self.cnt[on], (eng, on, val, self.cnt[on])
        if self.lastw[eng][on] >= val:
            return
        self.lastw[eng][on] = val
        self.ops[eng].append(("w", on, val))

    def wait_self_dma(self, eng, on, val):
        if val <= 0:
            return
        self.ops[eng].append(("w", on, val))

    def emit(self, eng, fn, inc=0, sem=None):
        tgt = sem if sem is not None else eng
        self.ops[eng].append(("o", fn, inc, tgt))
        if inc:
            self.cnt[tgt] += inc
        return self.cnt[tgt]

    def run(self, eng, h, sems):
        for item in self.ops[eng]:
            if item[0] == "w":
                h.wait_ge(sems[item[1]], item[2])
            else:
                ins = item[1](h)
                if item[2]:
                    ins.then_inc(sems[item[3]], item[2])


def build(debug=False):
    nc = bass.Bass("TRN2", target_bir_lowering=False)

    def din(name, shape, dtype=F32):
        return nc.dram_tensor(name, list(shape), dtype, kind="ExternalInput").ap()

    xT = din("xT", (D, T))
    wqk = din("wqk", (4, 128, 4096))
    wvr = din("wvr", (2, 128, 4096))
    vb = din("vb", (1, D))
    qkb = din("qkb", (128, 16))
    outw = din("outw", (2, 128, 4096))
    outb = din("outb", (128, 8))
    w12 = din("w12", (32, 128, 2048))
    s1c = din("s1c", (128, 32))
    s2c = din("s2c", (128, 32))
    h1p = din("h1p", (32, 128, 4096))
    s1hc = din("s1hc", (128, 32))
    h2p = din("h2p", (32, 128, 4096))
    s2hc = din("s2hc", (128, 32))
    w3p = din("w3p", (8, 128, 4096))
    s3c = din("s3c", (128, 8))
    gam = din("gam", (128, H))
    ident = din("ident", (128, 128), BF16)
    emat = din("emat", (128, 128))
    tri = din("tri", (128, 128))
    t1m = din("t1m", (128, 128))
    l0125 = din("l0125", (128, 1))
    c128 = din("c128", (128, 1))
    onesr = din("onesr", (1, 128))
    onesbf = din("onesbf", (1, 128), BF16)
    onescolb = din("onescolb", (128, 1), BF16)
    epsc = din("epsc", (128, 1))
    cvr = din("cvr", (128, 64))
    rcpd = din("rcpd", (1, 1))
    negh = din("negh", (128, 1))

    yT = nc.dram_tensor("yT", [D, T], F32, kind="ExternalOutput").ap()
    x1f = nc.dram_tensor("x1f", [D, T], F32, kind="Internal").ap()
    taps = {}
    if debug:
        for nm, shape, dt in (
            ("tapQT", (128, 4096), BF16), ("tapKT", (128, 4096), BF16),
            ("tapV", (128, 4096), BF16), ("tapYT", (128, 4096), BF16),
            ("tapR1", (128, 4096), F32), ("tapX1", (128, 16384), BF16),
            ("tapU", (128, 32768), BF16), ("tapH1", (128, 32768), BF16),
            ("tapMD", (128, 4096), F32), ("tapWG", (128, 32), F32),
            ("tapSB", (128, 1024), BF16),
        ):
            taps[nm] = nc.dram_tensor(nm, list(shape), dt, kind="ExternalOutput").ap()

    with ExitStack() as ctx:
        e = ctx.enter_context
        AA = e(nc.sbuf_tensor("AA", [128, 32768], BF16))
        AB = e(nc.sbuf_tensor("AB", [128, 32768], BF16))
        X1T = e(nc.sbuf_tensor("X1T", [128, 16384], BF16))
        PB = [e(nc.sbuf_tensor(f"PB{i}", [128, 4096], BF16)) for i in range(2)]
        XRS = e(nc.sbuf_tensor("XRS", [128, 1536], F32))
        TMPA = e(nc.sbuf_tensor("TMPA", [128, 1024], F32))
        RETT = e(nc.sbuf_tensor("RETT", [128, 256], BF16))
        STB = e(nc.sbuf_tensor("STB", [1, 2048], BF16))
        SF = e(nc.sbuf_tensor("SF", [128, 1024], F32))
        SB = e(nc.sbuf_tensor("SB", [128, 1024], BF16))
        VBr = e(nc.sbuf_tensor("VBr", [1, D], BF16))
        LOGG = e(nc.sbuf_tensor("LOGG", [128, 16], F32))
        GCC = e(nc.sbuf_tensor("GCC", [128, 16], F32))
        GS1 = e(nc.sbuf_tensor("GS1", [128, 16], F32))
        QKB = e(nc.sbuf_tensor("QKB", [128, 16], F32))
        OUTB = e(nc.sbuf_tensor("OUTB", [128, 8], F32))
        S1C = e(nc.sbuf_tensor("S1C", [128, 32], F32))
        S2C = e(nc.sbuf_tensor("S2C", [128, 32], F32))
        S1HC = e(nc.sbuf_tensor("S1HC", [128, 32], F32))
        S2HC = e(nc.sbuf_tensor("S2HC", [128, 32], F32))
        S3C = e(nc.sbuf_tensor("S3C", [128, 8], F32))
        IDENT = e(nc.sbuf_tensor("IDENT", [128, 128], BF16))
        EMAT = e(nc.sbuf_tensor("EMAT", [128, 128], F32))
        TRI = e(nc.sbuf_tensor("TRI", [128, 128], F32))
        T1M = e(nc.sbuf_tensor("T1M", [128, 128], F32))
        CVR = e(nc.sbuf_tensor("CVR", [128, 64], F32))
        QTILF = e(nc.sbuf_tensor("QTILF", [128, 1024], BF16))
        VTILF = e(nc.sbuf_tensor("VTILF", [128, 1024], BF16))
        KTOK2 = e(nc.sbuf_tensor("KTOK2", [128, 256], BF16))
        L0125 = e(nc.sbuf_tensor("L0125", [128, 1], F32))
        C128 = e(nc.sbuf_tensor("C128", [128, 1], F32))
        ONESR = e(nc.sbuf_tensor("ONESR", [1, 128], F32))
        ONESBF = e(nc.sbuf_tensor("ONESBF", [1, 128], BF16))
        ONESCB = e(nc.sbuf_tensor("ONESCB", [128, 1], BF16))
        EPSC = e(nc.sbuf_tensor("EPSC", [128, 1], F32))
        RCPD = e(nc.sbuf_tensor("RCPD", [1, 1], F32))
        NEGH = e(nc.sbuf_tensor("NEGH", [128, 1], F32))
        GAMR = e(nc.sbuf_tensor("GAMR", [128, 16], F32))
        GROW = e(nc.sbuf_tensor("GROW", [128, 16], F32))

        PS = [e(nc.psum_tensor(f"PS{i}", [128, 512], F32)) for i in range(6)]
        PK = [e(nc.psum_tensor(f"PK{i}", [128, 1024], BF16)) for i in range(2)]

        sems = {nm: e(nc.semaphore("sem_" + nm)) for nm in ALLSEMS}

        P = Prog()

        AAa = AA[:]
        ABa = AB[:]
        AAf = AAa.bitcast(F32)
        ABf = ABa.bitcast(F32)

        def xtb(c, tt0, tl):
            return AAa[:, c * 2048 + tt0: c * 2048 + tt0 + tl]

        def qtw(c):
            return AAa[:, 16384 + c * 512: 16384 + c * 512 + 512]

        def ktw(c):
            return AAa[:, 20480 + c * 512: 20480 + c * 512 + 512]

        def qhead(h, cl):
            p0 = (h % 2) * 64
            o = 16384 + (h // 2) * 512 + cl * 128
            return AAa[p0:p0 + 64, o: o + 128]

        def khead(h, cl):
            p0 = (h % 2) * 64
            o = 20480 + (h // 2) * 512 + cl * 128
            return AAa[p0:p0 + 64, o: o + 128]

        def vslice(tt, i0, il):
            return AAa[:, 24576 + tt * 1024 + i0: 24576 + tt * 1024 + i0 + il]

        def ytw(c):
            return AAa[:, 28672 + c * 512: 28672 + c * 512 + 512]

        def yhead(h, cl):
            p0 = (h % 2) * 64
            o = 28672 + (h // 2) * 512 + cl * 128
            return AAa[p0:p0 + 64, o: o + 128]

        def mmask(h):
            return ABf[:, h * 128: h * 128 + 128]

        def dfull3d():
            return ABf[:, 2048:3072].rearrange("p (c t) -> p c t", c=8)

        def wupfull():
            return ABf[:, 3072:4096]

        def qtilf(h, cl):
            p0 = (h % 2) * 64
            o = (h // 2) * 128
            return QTILF[p0:p0 + 64, o: o + 128]

        def vtilf(h, cl):
            return VTILF[:, h * 64: h * 64 + 64]

        def ktok2(cqpar, h):
            return KTOK2[:, cqpar * 128 + (h % 2) * 64: cqpar * 128 + (h % 2) * 64 + 64]

        def r1w(c):
            return ABf[:, 4096 + c * 512: 4096 + c * 512 + 512]

        def xfw(c):
            return ABf[:, 8192 + c * 512: 8192 + c * 512 + 512]

        def rbw(c):
            return ABa[:, 16384 + c * 512: 16384 + c * 512 + 512]

        def sqbw(c):
            return ABa[:, 20480 + c * 512: 20480 + c * 512 + 512]

        def outwb(oc, k0, kl):
            return ABa[:, 24576 + oc * 1024 + k0: 24576 + oc * 1024 + k0 + kl]

        def ub(oc, tt0, tl):
            return ABa[:, oc * 1024 + tt0: oc * 1024 + tt0 + tl]

        def h1b(oc, tt0, tl):
            return AAa[:, oc * 1024 + tt0: oc * 1024 + tt0 + tl]

        def h2b(oc, tt0, tl):
            return ABa[:, oc * 1024 + tt0: oc * 1024 + tt0 + tl]

        def r2b(oc, tt0, tl):
            return AAf[:, oc * 1024 + tt0: oc * 1024 + tt0 + tl]

        def outf(oc):
            return AAf[:, 8192 + oc * 512: 8192 + oc * 512 + 512]

        def xf2(oc):
            return AAf[:, 12288 + oc * 512: 12288 + oc * 512 + 512]

        def rb2(oc):
            return AAa[:, 24576 + oc * 512: 24576 + oc * 512 + 512]

        def sqb2(oc):
            return AAa[:, 28672 + oc * 512: 28672 + oc * 512 + 512]

        def x1t(c, tt0, tl):
            return X1T[:, c * 2048 + tt0: c * 2048 + tt0 + tl]

        def atil(par):
            return RETT[:, par * 128: par * 128 + 128]

        def sfh(ping, h):
            p0 = (h % 2) * 64
            return SF[p0:p0 + 64, ping * 512 + (h // 2) * 64: ping * 512 + (h // 2) * 64 + 64]

        def sbh(ping, h):
            p0 = (h % 2) * 64
            return SB[p0:p0 + 64, ping * 512 + (h // 2) * 64: ping * 512 + (h // 2) * 64 + 64]

        def xrs(sl):
            return XRS[:, sl * 512: sl * 512 + 512]

        def outslot(idx):
            sl = idx % 5
            if sl < 3:
                return XRS[:, sl * 512: sl * 512 + 512]
            if sl == 3:
                return QTILF[:].bitcast(F32)
            return VTILF[:].bitcast(F32)

        def tmpa(hh):
            return TMPA[:, hh * 512: hh * 512 + 512]

        ST0 = TMPA[0:1, 0:512]
        ST1 = TMPA[0:1, 512:1024]

        for dst, src in (
            (QKB[:], qkb), (OUTB[:], outb), (S1C[:], s1c), (S2C[:], s2c),
            (S1HC[:], s1hc), (S2HC[:], s2hc), (S3C[:], s3c),
            (IDENT[:], ident), (EMAT[:], emat), (TRI[:], tri), (T1M[:], t1m),
            (L0125[:], l0125), (C128[:], c128),
            (ONESR[:], onesr), (ONESBF[:], onesbf), (ONESCB[:], onescolb),
            (EPSC[:], epsc), (RCPD[:], rcpd), (NEGH[:], negh), (GAMR[:], gam), (CVR[:], cvr),
        ):
            P.emit("q", lambda h, d=dst, s=src: h.dma_start(out=d, in_=s), 16, sem="qi")
        q_small = P.cnt["qi"]

        P.emit("g", lambda h: h.dma_start(out=VBr[:], in_=vb), 16, sem="gi")
        P.emit("g", lambda h: h.dma_start(
            out=AAa[:, 0:16384].rearrange("p (c t) -> p c t", c=8),
            in_=xT.rearrange("(c p) t -> p c t", p=128)), 16, sem="gi")
        for i in range(2):
            P.emit("g", lambda h, i=i: h.dma_start(
                out=ABa[:, 24576 + i * 4096: 24576 + (i + 1) * 4096], in_=outw[i]), 16, sem="gi")
        g_init = P.cnt["gi"]

        P.wait("s", "qi", q_small)
        P.emit("s", lambda h: h.activation(GROW[:], GAMR[:], AF.Sigmoid), 1)
        P.wait_self_dma("s", "s", P.cnt["s"])
        P.emit("s", lambda h: h.activation(LOGG[:], GROW[:], AF.Ln), 1)
        s_lg = P.cnt["s"]
        P.wait("v", "s", s_lg)
        P.wait("v", "qi", q_small)
        for h_ in range(H):
            P.emit("v", lambda h, hh=h_: h.tensor_scalar(
                ABf[:, 4096 + hh * 128: 4096 + hh * 128 + 128], EMAT[:],
                LOGG[:, hh:hh + 1], None, ALU.mult), 1 if h_ == H - 1 else 0)
        v_mscr = P.cnt["v"]
        P.wait("s", "v", v_mscr)
        P.emit("s", lambda h: h.activation(ABf[:, 6144:8192], ABf[:, 4096:6144], AF.Exp), 1)
        s_mexp = P.cnt["s"]
        P.wait("v", "s", s_mexp)
        for h_ in range(H):
            P.emit("v", lambda h, hh=h_: h.tensor_tensor(
                mmask(hh), ABf[:, 6144 + hh * 128: 6144 + hh * 128 + 128], TRI[:], ALU.mult))
        for cq in range(8):
            for half in range(2):
                hh_ = 2 * cq + half
                P.emit("v", lambda h, cq=cq, half=half, hh_=hh_: h.tensor_scalar(
                    ABf[half * 64:half * 64 + 64, 4096 + cq * 128: 4096 + cq * 128 + 128],
                    T1M[half * 64:half * 64 + 64, :],
                    LOGG[half * 64:half * 64 + 64, hh_:hh_ + 1], None, ALU.mult))
        for h_ in range(H):
            P.emit("v", lambda h, hh=h_: h.tensor_scalar(
                ABf[:, 5120 + hh * 64: 5120 + hh * 64 + 64], CVR[:],
                LOGG[:, hh:hh + 1], None, ALU.mult), 1 if h_ == H - 1 else 0)
        v_dscr = P.cnt["v"]
        P.wait("s", "v", v_dscr)
        P.emit("s", lambda h: h.activation(ABf[:, 2048:3072], ABf[:, 4096:5120], AF.Exp))
        P.emit("s", lambda h: h.activation(ABf[:, 3072:4096], ABf[:, 5120:6144], AF.Exp, bias=L0125[:]), 1)
        P.emit("v", lambda h: h.tensor_scalar(GS1[:], LOGG[:], C128[:], None, ALU.mult), 1)
        v_gs2 = P.cnt["v"]
        P.wait("s", "v", v_gs2)
        P.emit("s", lambda h: h.activation(GCC[:], GS1[:], AF.Exp), 1)
        P.emit("v", lambda h: h.memset(SF[:], 0.0))
        P.emit("v", lambda h: h.memset(SB[:], 0.0), 1)
        s_init = P.cnt["s"]
        v_init = P.cnt["v"]

        if debug:
            P.wait("q", "s", s_init)
            P.wait("q", "v", v_init)
            P.emit("q", lambda h: h.dma_start(out=taps["tapMD"], in_=ABf[:, 0:4096]), 16, sem="qi")
            P.emit("q", lambda h: h.dma_start(out=taps["tapWG"][:, 16:32], in_=GCC[:]), 16, sem="qi")

        gslot = [0]
        pb_free_t = {}
        ret_info, w_of = [], []
        mm1_cnt, v1_cnt, back_cnt, s1_cnt, v4_cnt = {}, {}, {}, {}, {}
        s_pk, sb_cast, prep_cnt = {}, {}, {}
        qk_evac = {}
        v_evac = {}
        ret_v_done = {}
        ret_s_done = {}
        op_t = {}
        opv_last = {}
        ln1_v = {}
        x1_spill_q = {-1: 0}

        def gdma(fn):
            key = gslot[0]
            if key >= 2:
                P.wait("g", "t", pb_free_t.get(key - 2, 0))
            sl = key % 2
            gslot[0] += 1
            cnt = P.emit("g", lambda h, sl=sl, fn=fn: fn(h, sl), 16, sem="g%d" % sl)
            return key, sl, cnt

        def layer_norm(P, reg_in, reg_out, reg_scr, stats_src, n_sl, t_prewait=None, bulk=None, scr2=None, banks=None, strows=None, prev=None, s_stat_pre=None):
            rb, sqb = stats_src
            bk0, bk1 = banks if banks is not None else (PS[2][:], PS[3][:])
            st0, st1 = strows if strows is not None else (ST0, ST1)
            prev_tstats, prev_varf, prev_tbc, prev_apply = prev if prev is not None else (0, 0, 0, 0)
            if s_stat_pre is not None:
                s_stat = s_stat_pre
            else:
                P.wait("s", "t", prev_tstats)
                if bulk is not None:
                    bin_, brb, bsqb = bulk
                    P.emit("s", lambda h: h.copy(brb, bin_))
                    P.emit("s", lambda h: h.activation(bsqb, bin_, AF.Square), 1)
                else:
                    for oc in range(8):
                        P.emit("s", lambda h, oc=oc: h.copy(rb(oc), reg_in(oc)))
                        P.emit("s", lambda h, oc=oc: h.activation(sqb(oc), reg_in(oc), AF.Square),
                               1 if oc == 7 else 0)
                s_stat = P.cnt["s"]
            P.wait("t", "s", s_stat)
            P.wait("t", "v", prev_varf)
            for tv in t_prewait or []:
                P.wait("t", tv[0], tv[1])
            for oc in range(8):
                P.emit("t", lambda h, oc=oc, bk0=bk0: h.matmul(bk0[0:1, 0:512], ONESCB[:], rb(oc),
                                                      start=(oc == 0), stop=(oc == 7)), 1)
            for oc in range(8):
                P.emit("t", lambda h, oc=oc, bk1=bk1: h.matmul(bk1[0:1, 0:512], ONESCB[:], sqb(oc),
                                                      start=(oc == 0), stop=(oc == 7)), 1)
            t_stats = P.cnt["t"]
            P.wait("v", "t", t_stats)
            P.wait("v", "t", prev_tbc)
            P.wait_self_dma("v", "v", prev_apply)
            P.emit("v", lambda h, bk0=bk0, st0=st0: h.tensor_scalar(st0, bk0[0:1, 0:512], RCPD[:], None, ALU.mult))
            P.emit("v", lambda h, bk1=bk1, st1=st1: h.tensor_scalar(st1, bk1[0:1, 0:512], RCPD[:], None, ALU.mult), 1)
            P.wait_self_dma("v", "v", P.cnt["v"])
            P.emit("v", lambda h, st0=st0: h.tensor_copy(STB[0:1, 0:512], st0))
            P.emit("v", lambda h, st1=st1: h.tensor_copy(STB[0:1, 1024:1536], st1), 1)
            P.wait_self_dma("v", "v", P.cnt["v"])
            P.emit("v", lambda h, st0=st0: h.tensor_sub(XRS[0:1, 0:512], st0, STB[0:1, 0:512]))
            P.emit("v", lambda h, st1=st1: h.tensor_sub(XRS[0:1, 512:1024], st1, STB[0:1, 1024:1536]), 1)
            P.wait_self_dma("v", "v", P.cnt["v"])
            P.emit("v", lambda h: h.tensor_copy(STB[0:1, 512:1024], XRS[0:1, 0:512]))
            P.emit("v", lambda h: h.tensor_copy(STB[0:1, 1536:2048], XRS[0:1, 512:1024]), 1)
            v_st = P.cnt["v"]
            P.wait("t", "v", v_st)
            P.emit("t", lambda h, bk0=bk0: h.matmul(bk0, ONESBF[:], STB[0:1, 0:512], start=True, stop=False))
            P.emit("t", lambda h, bk0=bk0: h.matmul(bk0, ONESBF[:], STB[0:1, 512:1024], start=False, stop=True), 1)
            P.emit("t", lambda h, bk1=bk1: h.matmul(bk1, ONESBF[:], STB[0:1, 1024:1536], start=True, stop=False))
            P.emit("t", lambda h, bk1=bk1: h.matmul(bk1, ONESBF[:], STB[0:1, 1536:2048], start=False, stop=True), 1)
            t_bc = P.cnt["t"]
            mb, msq = scr2 if scr2 is not None else (tmpa(0), tmpa(1))
            P.wait("v", "t", t_bc)
            P.emit("v", lambda h, mb=mb, bk0=bk0: h.tensor_copy(mb, bk0), 1)
            P.wait_self_dma("v", "v", P.cnt["v"])
            P.emit("v", lambda h, mb=mb, msq=msq: h.tensor_mul(msq, mb, mb), 1)
            P.wait_self_dma("v", "v", P.cnt["v"])
            P.emit("v", lambda h, msq=msq, bk1=bk1: h.tensor_sub(xrs(0), bk1, msq), 1)
            v_varf = P.cnt["v"]
            P.wait("s", "v", v_varf)
            P.wait("s", "v", prev_apply)
            P.emit("s", lambda h: h.activation(xrs(1), xrs(0), AF.Ln, bias=EPSC[:]), 1)
            P.wait_self_dma("s", "s", P.cnt["s"])
            P.emit("s", lambda h: h.activation(xrs(2), xrs(1), AF.Exp, scale=NEGH[:]), 1)
            s_sqrt = P.cnt["s"]
            P.wait("v", "s", s_sqrt)
            mbrs = SF[:, 0:512].bitcast(F32).bitcast(F32) if False else SF[:, 0:512]
            P.emit("v", lambda h, mb=mb, mbrs=mbrs: h.tensor_mul(mbrs, mb, xrs(2)), 1)
            t1 = [SF[:, 512:1024], scr2[1], scr2[0]]
            P.wait_self_dma("v", "v", P.cnt["v"])
            mval, sval = {}, {}
            for oc in range(8):
                if oc >= 3:
                    P.wait_self_dma("v", "v", sval[oc - 3])
                mval[oc] = P.emit("v", lambda h, oc=oc, t1=t1: h.tensor_mul(
                    t1[oc % 3], reg_in(oc), xrs(2)), 1)
                if oc >= 1:
                    P.wait_self_dma("v", "v", mval[oc - 1])
                    sval[oc - 1] = P.emit("v", lambda h, oc=oc, t1=t1, mbrs=mbrs: h.tensor_sub(
                        reg_out(oc - 1), t1[(oc - 1) % 3], mbrs), 1)
            P.wait_self_dma("v", "v", mval[7])
            P.emit("v", lambda h, t1=t1, mbrs=mbrs: h.tensor_sub(reg_out(7), t1[7 % 3], mbrs), 1)
            svl = [sval[i] for i in range(7)] + [P.cnt["v"]]
            return P.cnt["v"], v_varf, svl

        ln1_varf = {-1: 0}
        ln1_cast_s = {-1: 0}
        ln1_sub_v = {-1: 0}
        ln1_tmp = {}

        def ln1_stage_a(w):
            P.wait("s", "v", opv_last[w])
            P.emit("s", lambda h: h.copy(ABa[:, 16384:20480], ABf[:, 4096:8192]))
            P.emit("s", lambda h: h.activation(ABa[:, 20480:24576], ABf[:, 4096:8192], AF.Square), 1)
            ln1_tmp[("s_stat", w)] = P.cnt["s"]

        def ln1_stage_b(w):
            bk0, bk1 = PS[2][:], PS[3][:]
            st0, st1 = ST0, ST1
            prev_tbc = 0
            P.wait("t", "s", ln1_tmp[("s_stat", w)])
            P.wait("t", "s", v_evac.get((w + 1, 7), v_evac[(w, 7)]))
            for oc in range(8):
                P.emit("t", lambda h, oc=oc: h.matmul(PS[2][0:1, 0:512], ONESCB[:], rbw(oc),
                                                      start=(oc == 0), stop=(oc == 7)), 1)
            for oc in range(8):
                P.emit("t", lambda h, oc=oc: h.matmul(PS[3][0:1, 0:512], ONESCB[:], sqbw(oc),
                                                      start=(oc == 0), stop=(oc == 7)), 1)
            t_stats = P.cnt["t"]
            P.wait("v", "t", t_stats)
            P.wait("v", "t", prev_tbc)
            P.emit("v", lambda h, bk0=bk0, st0=st0: h.tensor_scalar(st0, bk0[0:1, 0:512], RCPD[:], None, ALU.mult))
            P.emit("v", lambda h, bk1=bk1, st1=st1: h.tensor_scalar(st1, bk1[0:1, 0:512], RCPD[:], None, ALU.mult), 1)
            P.wait_self_dma("v", "v", P.cnt["v"])
            P.emit("v", lambda h, st0=st0: h.tensor_copy(STB[0:1, 0:512], st0))
            P.emit("v", lambda h, st1=st1: h.tensor_copy(STB[0:1, 1024:1536], st1), 1)
            P.wait_self_dma("v", "v", P.cnt["v"])
            P.emit("v", lambda h, st0=st0: h.tensor_sub(XRS[0:1, 0:512], st0, STB[0:1, 0:512]))
            P.emit("v", lambda h, st1=st1: h.tensor_sub(XRS[0:1, 512:1024], st1, STB[0:1, 1024:1536]), 1)
            P.wait_self_dma("v", "v", P.cnt["v"])
            P.emit("v", lambda h: h.tensor_copy(STB[0:1, 512:1024], XRS[0:1, 0:512]))
            P.emit("v", lambda h: h.tensor_copy(STB[0:1, 1536:2048], XRS[0:1, 512:1024]), 1)
            v_st = P.cnt["v"]
            P.wait("t", "v", v_st)
            P.emit("t", lambda h, bk0=bk0: h.matmul(bk0, ONESBF[:], STB[0:1, 0:512], start=True, stop=False))
            P.emit("t", lambda h, bk0=bk0: h.matmul(bk0, ONESBF[:], STB[0:1, 512:1024], start=False, stop=True), 1)
            P.emit("t", lambda h, bk1=bk1: h.matmul(bk1, ONESBF[:], STB[0:1, 1024:1536], start=True, stop=False))
            P.emit("t", lambda h, bk1=bk1: h.matmul(bk1, ONESBF[:], STB[0:1, 1536:2048], start=False, stop=True), 1)
            ln1_tmp[("t_bc", w)] = P.cnt["t"]

        def ln1_stage_c1(w):
            bk0, bk1 = PS[2][:], PS[3][:]
            P.wait("v", "t", ln1_tmp[("t_bc", w)])
            P.emit("v", lambda h: h.tensor_copy(tmpa(0), PS[2][:]), 1)
            P.wait_self_dma("v", "v", P.cnt["v"])
            P.emit("v", lambda h: h.tensor_mul(tmpa(1), tmpa(0), tmpa(0)), 1)
            P.wait_self_dma("v", "v", P.cnt["v"])
            P.emit("v", lambda h: h.tensor_sub(xrs(0), PS[3][:], tmpa(1)), 1)
            v_vf = P.cnt["v"]
            ln1_varf[w] = v_vf
            P.wait("s", "v", v_vf)
            P.emit("s", lambda h: h.activation(xrs(1), xrs(0), AF.Ln, bias=EPSC[:]), 1)
            P.wait_self_dma("s", "s", P.cnt["s"])
            P.emit("s", lambda h: h.activation(xrs(2), xrs(1), AF.Exp, scale=NEGH[:]), 1)
            ln1_tmp[("s_rs", w)] = P.cnt["s"]

        def ln1_stage_c2(w):
            P.wait("v", "s", ln1_tmp[("s_rs", w)])
            for oc in range(8):
                P.emit("v", lambda h, oc=oc: h.tensor_sub(xfw(oc), r1w(oc), tmpa(0)),
                       1 if oc == 7 else 0)
            ln1_sub_v[w] = P.cnt["v"]

        def ln1_stage_c3(w):
            t0 = w * TW
            P.wait_self_dma("v", "v", ln1_sub_v[w])
            for oc in range(8):
                P.emit("v", lambda h, oc=oc: h.tensor_mul(r1w(oc), xfw(oc), xrs(2)),
                       1 if oc == 7 else 0)
            v_x1 = P.cnt["v"]
            ln1_v[w] = v_x1
            P.wait("s", "v", v_x1)
            for oc in range(8):
                P.emit("s", lambda h, oc=oc, t0=t0: h.copy(x1t(oc, t0, TW), r1w(oc)),
                       1 if oc == 7 else 0)
            ln1_cast_s[w] = P.cnt["s"]

        def ln1_stage_c(w):
            ln1_stage_c1(w)
            ln1_stage_c2(w)
            ln1_stage_c3(w)

        for w in range(NW):
            t0 = w * TW
            wkeys = {}

            def wdma(idx, wkeys=None):
                if idx < 4:
                    fn = lambda h, sl, gi=idx: h.dma_start(out=PB[sl][:], in_=wqk[gi])
                else:
                    fn = lambda h, sl, vi=idx - 4: h.dma_start(out=PB[sl][:], in_=wvr[vi])
                wkeys[idx] = gdma(fn)

            wdma(0, wkeys)
            wdma(1, wkeys)
            if w > 0:
                ln1_stage_a(w - 1)

            prev_ret_v = ret_v_done.get(w - 1, 0)
            prev_op_t = op_t.get(w - 1, 0)
            prev_opv = opv_last.get(w - 1, 0)

            for ot in range(16):
                gi = ot // 4
                key, sl, gcnt = wkeys[gi]
                bank = ot % 4
                P.wait("t", "g%d" % sl, gcnt)
                if w == 0 and ot == 0:
                    P.wait("t", "v", v_init)
                    P.wait("t", "s", s_init)
                    P.wait("t", "gi", g_init)
                if ot < 4:
                    P.wait("t", "v", prev_opv)
                else:
                    P.wait("t", "v", qk_evac[(w, ot - 4)])
                for c in range(8):
                    P.emit("t", lambda h, ot=ot, c=c, sl=sl, bank=bank, t0=t0:
                           h.matmul(PS[bank][:],
                                    PB[sl][:, (ot % 4) * 1024 + c * 128:(ot % 4) * 1024 + c * 128 + 128],
                                    xtb(c, t0, TW), start=(c == 0), stop=(c == 7)), 1)
                pb_free_t[key] = P.cnt["t"]
                t_tile = P.cnt["t"]
                P.wait("v", "t", t_tile)
                if w > 0:
                    P.wait("v", "t", prev_op_t)
                dst = qtw(ot % 8) if ot < 8 else ktw(ot % 8)
                P.emit("v", lambda h, dst=dst, bank=bank, ot=ot:
                       h.tensor_scalar(dst, PS[bank][:], QKB[:, ot:ot + 1], None, ALU.add), 1)
                qk_evac[(w, ot)] = P.cnt["v"]
                if ot % 4 == 3 and ot // 4 + 2 <= 5:
                    wdma(ot // 4 + 2, wkeys)

            for tt in range(4):
                for ih in range(2):
                    bank = (tt * 2 + ih) % 4
                    P.wait("t", "v", qk_evac[(w, 12 + bank)])
                    if tt >= 2:
                        P.wait("t", "s", v_evac[(w, (tt - 2) * 2 + ih)])
                    for cc in range(8):
                        gk = wkeys[4 + cc // 4]
                        P.wait("t", "g%d" % gk[1], gk[2])
                        P.emit("t", lambda h, tt=tt, ih=ih, cc=cc, bank=bank, sl=gk[1], t0=t0:
                               h.matmul(PS[bank][:], xtb(cc, t0 + tt * 128, 128),
                                        PB[sl][:, (cc % 4) * 1024 + ih * 512:(cc % 4) * 1024 + ih * 512 + 512],
                                        start=(cc == 0), stop=False), 1)
                    P.wait("t", "gi", g_init)
                    P.emit("t", lambda h, ih=ih, bank=bank:
                           h.matmul(PS[bank][:], ONESBF[:], VBr[0:1, ih * 512: ih * 512 + 512],
                                    start=False, stop=True), 1)
                    t_vt = P.cnt["t"]
                    P.wait("s", "t", t_vt)
                    if w > 0:
                        P.wait("s", "t", prev_op_t)
                    P.emit("s", lambda h, tt=tt, ih=ih, bank=bank:
                           h.copy(vslice(tt, ih * 512, 512), PS[bank][:]), 1)
                    v_evac[(w, tt * 2 + ih)] = P.cnt["s"]
            pb_free_t[wkeys[4][0]] = P.cnt["t"]
            pb_free_t[wkeys[5][0]] = P.cnt["t"]

            if w > 0:
                ln1_stage_b(w - 1)

            if debug and w == 0:
                P.wait("q", "v", P.cnt["v"])
                P.emit("q", lambda h: h.dma_start(out=taps["tapQT"], in_=AAa[:, 16384:20480]), 16, sem="qi")
                P.emit("q", lambda h: h.dma_start(out=taps["tapKT"], in_=AAa[:, 20480:24576]), 16, sem="qi")
                P.emit("q", lambda h: h.dma_start(out=taps["tapV"], in_=AAa[:, 24576:28672]), 16, sem="qi")
                P.wait("v", "qi", P.cnt["qi"])
                P.wait("s", "qi", P.cnt["qi"])

            def emit_back(j):
                cl_j, h_j = ret_info[j]
                cglob_j = w_of[j] * 4 + cl_j
                par = j % 2
                p0 = (h_j % 2) * 64
                ping = cglob_j % 2
                prev = 1 - ping
                cqpar = (h_j // 2) % 2
                P.wait("t", "v", v1_cnt[j])
                P.wait("t", "v", prep_cnt[cglob_j])
                P.wait("t", "s", s_pk[(cglob_j * 8 + h_j // 2)])
                P.wait("t", "s", v_evac[(w_of[j], cl_j * 2 + h_j // 8)])
                if cglob_j > 0:
                    P.wait("t", "v", sb_cast[(cglob_j - 1, h_j // 8)])
                if j >= 2:
                    P.wait("t", "s", s1_cnt[j - 2])
                    P.wait("t", "v", v4_cnt[j - 2])
                P.emit("t", lambda h, h_j=h_j, cl_j=cl_j, par=par, prev=prev, p0=p0:
                       h.matmul(PS[par][p0:p0 + 64, 0:128], sbh(prev, h_j), qtilf(h_j, cl_j),
                                start=True, stop=False, tile_position=(p0, p0)), 1)
                P.emit("t", lambda h, h_j=h_j, cl_j=cl_j, par=par, p0=p0:
                       h.matmul(PS[par][p0:p0 + 64, 0:128], vslice(cl_j, h_j * 64, 64), atil(par),
                                start=False, stop=True, tile_position=(0, p0)), 1)
                P.emit("t", lambda h, h_j=h_j, cl_j=cl_j, par=par, p0=p0, cqpar=cqpar:
                       h.matmul(PS[par][p0:p0 + 64, 128:192], ktok2(cqpar, h_j), vtilf(h_j, cl_j),
                                start=True, stop=True, tile_position=(0, p0)), 1)
                back_cnt[j] = P.cnt["t"]
                P.wait("s", "t", back_cnt[j])
                P.emit("s", lambda h, h_j=h_j, cl_j=cl_j, par=par, p0=p0:
                       h.copy(yhead(h_j, cl_j), PS[par][p0:p0 + 64, 0:128]), 1)
                s1_cnt[j] = P.cnt["s"]
                P.wait("v", "t", back_cnt[j])
                P.wait("v", "s", s1_cnt[j])
                P.emit("v", lambda h, h_j=h_j, par=par, ping=ping, prev=prev, p0=p0:
                       h.scalar_tensor_tensor(sfh(ping, h_j), sfh(prev, h_j),
                                              GCC[p0:p0 + 64, h_j:h_j + 1],
                                              PS[par][p0:p0 + 64, 128:192],
                                              ALU.mult, ALU.add), 1)
                v4_cnt[j] = P.cnt["v"]
                if h_j in (7, 15):
                    half = h_j // 8
                    P.wait_self_dma("v", "v", P.cnt["v"])
                    P.emit("v", lambda h, ping=ping, half=half: h.tensor_copy(
                        SB[:, ping * 512 + half * 256: ping * 512 + half * 256 + 256],
                        SF[:, ping * 512 + half * 256: ping * 512 + half * 256 + 256]), 1)
                    sb_cast[(cglob_j, half)] = P.cnt["v"]

            for cl in range(4):
                cglob = w * 4 + cl
                for hh in range(H):
                    if cl == 3 and w > 0:
                        if hh == 0:
                            ln1_stage_c1(w - 1)
                        elif hh == 6:
                            ln1_stage_c2(w - 1)
                        elif hh == 12:
                            ln1_stage_c3(w - 1)
                    if hh == (1 if cglob > 0 else 0):
                        if cglob >= 1 and (cglob - 1) * 16 + 15 in back_cnt:
                            P.wait("v", "t", back_cnt[(cglob - 1) * 16 + 15])
                        P.wait("v", "s", v_evac[(w, cl * 2 + 1)])
                        P.emit("v", lambda h, cl=cl: h.tensor_tensor(
                            QTILF[:].rearrange("p (c t) -> p c t", c=8),
                            AAa[:, 16384:20480].rearrange("p (c t) -> p c t", c=8)[:, :, cl * 128: cl * 128 + 128],
                            dfull3d(), ALU.mult))
                        P.emit("v", lambda h, cl=cl: h.tensor_tensor(
                            VTILF[:],
                            AAa[:, 24576 + cl * 1024: 24576 + cl * 1024 + 1024],
                            wupfull(), ALU.mult), 1)
                        prep_cnt[cglob] = P.cnt["v"]
                    u = len(ret_info)
                    ret_info.append((cl, hh))
                    w_of.append(w)
                    par = u % 2
                    p0 = (hh % 2) * 64
                    P.wait("t", "v", qk_evac[(w, hh // 2)])
                    P.wait("t", "v", qk_evac[(w, 8 + hh // 2)])
                    if u >= 2:
                        P.wait("t", "v", v1_cnt[u - 2])
                    if hh % 2 == 0:
                        gpair = cglob * 8 + hh // 2
                        cqpar = (hh // 2) % 2
                        if gpair >= 2:
                            P.wait("t", "s", s_pk[gpair - 2])
                        P.emit("t", lambda h, hh=hh, cl=cl, cqpar=cqpar:
                               h.transpose(PK[cqpar][:, 0:128],
                                           AAa[:, 20480 + (hh // 2) * 512 + cl * 128:
                                               20480 + (hh // 2) * 512 + cl * 128 + 128],
                                           IDENT[:]), 1)
                        t_trans = P.cnt["t"]
                        P.wait("s", "t", t_trans)
                        if gpair >= 2 and (gpair - 2) * 2 + 1 in back_cnt:
                            P.wait("s", "t", back_cnt[(gpair - 2) * 2 + 1])
                        P.emit("s", lambda h, cqpar=cqpar:
                               h.copy(KTOK2[:, cqpar * 128: cqpar * 128 + 128],
                                      PK[cqpar][:, 0:128]), 1)
                        s_pk[gpair] = P.cnt["s"]
                    P.emit("t", lambda h, hh=hh, cl=cl, par=par, p0=p0:
                           h.matmul(PS[4 + par][:, 0:128], khead(hh, cl), qhead(hh, cl),
                                    start=True, stop=True, tile_position=(p0, 0)), 1)
                    mm1_cnt[u] = P.cnt["t"]
                    P.wait("v", "t", mm1_cnt[u])
                    P.emit("v", lambda h, hh=hh, par=par:
                           h.tensor_tensor(atil(par), PS[4 + par][:, 0:128], mmask(hh), ALU.mult), 1)
                    v1_cnt[u] = P.cnt["v"]
                    if u >= 1 and back_cnt.get(u - 1) is None:
                        emit_back(u - 1)
            emit_back(len(ret_info) - 1)
            ret_v_done[w] = P.cnt["v"]
            ret_s_done[w] = P.cnt["s"]

            if debug and w == 0:
                P.wait("q", "s", ret_s_done[0])
                P.wait("q", "v", ret_v_done[0])
                P.emit("q", lambda h: h.dma_start(out=taps["tapYT"], in_=AAa[:, 28672:32768]), 16, sem="qi")
                P.emit("q", lambda h: h.dma_start(out=taps["tapSB"], in_=SB[:]), 16, sem="qi")
                P.wait("s", "qi", P.cnt["qi"])
                P.wait("v", "qi", P.cnt["qi"])

            P.wait("t", "gi", g_init)
            opevac = {}
            for oc in range(8):
                bank = oc % 4
                if oc >= 3:
                    P.wait("q", "v", opevac[oc - 3])
                else:
                    P.wait("q", "v", ln1_v.get(w - 1, 0))
                P.emit("q", lambda h, oc=oc, t0=t0: h.dma_start(
                    out=xrs(oc % 3), in_=xT[oc * 128:(oc + 1) * 128, t0:t0 + TW]), 16,
                    sem="q%d" % (oc % 3))
                q_rr = ("q%d" % (oc % 3), P.cnt["q%d" % (oc % 3)])
                P.wait("t", "s", ret_s_done[w])
                P.wait("t", "v", ret_v_done[w])
                P.wait("t", "v", ln1_varf.get(w - 1, 0))
                if oc >= 4:
                    P.wait("t", "v", opevac[oc - 4])
                for c in range(8):
                    P.emit("t", lambda h, oc=oc, c=c, bank=bank:
                           h.matmul(PS[bank][:], outwb(oc, c * 128, 128), ytw(c),
                                    start=(c == 0), stop=(c == 7)), 1)
                t_tile = P.cnt["t"]
                P.wait("v", "t", t_tile)
                P.wait("v", q_rr[0], q_rr[1])
                P.wait("v", "s", ln1_cast_s.get(w - 1, 0))
                P.emit("v", lambda h, oc=oc, bank=bank:
                       h.scalar_tensor_tensor(r1w(oc), PS[bank][:], OUTB[:, oc:oc + 1],
                                              xrs(oc % 3), ALU.add, ALU.add), 1)
                opevac[oc] = P.cnt["v"]
            op_t[w] = P.cnt["t"]
            opv_last[w] = opevac[7]
            if debug and w == 0:
                P.wait("q", "v", opevac[7])
                P.emit("q", lambda h: h.dma_start(out=taps["tapR1"], in_=ABf[:, 4096:8192]), 16, sem="qi")
                P.wait("v", "qi", P.cnt["qi"])
                P.wait("s", "qi", P.cnt["qi"])
            if w == NW - 1:
                ln1_stage_a(w)
                ln1_stage_b(w)
                ln1_stage_c(w)

        if debug:
            P.wait("q", "s", P.cnt["s"])
            P.emit("q", lambda h: h.dma_start(out=taps["tapX1"], in_=X1T[:]), 16, sem="qi")

        att_s = P.cnt["s"]
        ffn_state = {"store_q": 0, "ln2_v": 0, "out_v": {}}
        store_q = 0
        ln2_v = 0
        prev_out_v = 0
        PKF = [PK[i][:].bitcast(F32) for i in range(2)]

        store_guard = {0: 0, 1: 0}

        def do_ln2(tb, ths=(0, 1)):
            nonlocal store_q, ln2_v
            out_v_tb = ffn_state["out_v"][tb]
            prev_vout = ffn_state.get("ln2_pv", 0)
            prev_tstats = ffn_state.get("ln2_pt", 0)
            prev_varf = ffn_state.get("ln2_pvf", 0)
            yTr = yT.rearrange("(c p) t -> p c t", p=128)
            cfg = {
                0: (outf, (rb2, sqb2),
                    (AAf[:, 0:8192].rearrange("p (c t) -> p c t", c=8)[:, :, 0:512],
                     AAa[:, 24576:28672].rearrange("p (c t) -> p c t", c=8),
                     AAa[:, 28672:32768].rearrange("p (c t) -> p c t", c=8)),
                    AAf[:, 8192:12288].rearrange("p (c t) -> p c t", c=8)),
                1: ((lambda oc: r2b(oc, 0, 512)),
                    ((lambda oc, tb=tb: X1T[:, oc * 2048 + tb * 1024: oc * 2048 + tb * 1024 + 512]),
                     (lambda oc, tb=tb: X1T[:, oc * 2048 + tb * 1024 + 512: oc * 2048 + tb * 1024 + 1024])),
                    (AAf[:, 0:8192].rearrange("p (c t) -> p c t", c=8)[:, :, 512:1024],
                     X1T[:].rearrange("p (c t) -> p c t", c=8)[:, :, tb * 1024: tb * 1024 + 512],
                     X1T[:].rearrange("p (c t) -> p c t", c=8)[:, :, tb * 1024 + 512: tb * 1024 + 1024]),
                    AAf[:, 0:8192].rearrange("p (c t) -> p c t", c=8)[:, :, 0:512]),
            }
            P.wait("s", "v", out_v_tb)
            P.wait("s", "t", prev_tstats)
            pre = {}
            for th in ths:
                bin_, brb, bsqb = cfg[th][2]
                P.emit("s", lambda h, bin_=bin_, brb=brb: h.copy(brb, bin_))
                P.emit("s", lambda h, bin_=bin_, bsqb=bsqb: h.activation(bsqb, bin_, AF.Square), 1)
                pre[th] = P.cnt["s"]
            for th in ths:
                ro, src, blk, sin = cfg[th]
                if tb == 1 and th == 0:
                    P.wait("v", "qs", P.cnt["qs"])
                v_out, _vf2, svl = layer_norm(
                    P,
                    (lambda oc, th=th: r2b(oc, th * 512, 512)),
                    ro, xf2, src, 3,
                    bulk=blk, s_stat_pre=pre[th],
                    scr2=(QTILF[:].bitcast(F32), VTILF[:].bitcast(F32)),
                    banks=(PKF[0][:, 0:512], PKF[1][:, 0:512]),
                    strows=(SF[0:1, 0:512], SF[0:1, 512:1024]),
                    prev=(prev_tstats, prev_varf, prev_tstats, prev_vout))
                prev_vout = v_out
                prev_varf = _vf2
                prev_tstats = P.cnt["t"]
                if tb == 1:
                    for oc in range(8):
                        P.wait("q", "v", svl[oc])
                        P.emit("q", lambda h, tb=tb, th=th, oc=oc, sin=sin, yTr=yTr: h.dma_start(
                            out=yTr[:, oc:oc + 1, tb * TB + th * 512: tb * TB + th * 512 + 512],
                            in_=sin[:, oc:oc + 1, 0:512]), 16, sem="qs")
                else:
                    P.wait("q", "v", v_out)
                    P.emit("q", lambda h, tb=tb, th=th, sin=sin, yTr=yTr: h.dma_start(
                        out=yTr[:, :, tb * TB + th * 512: tb * TB + th * 512 + 512],
                        in_=sin), 16, sem="qs")
                store_q = P.cnt["qs"]
                store_guard[th] = store_q
                ffn_state["ln2_pv"] = prev_vout
                ffn_state["ln2_pt"] = prev_tstats
                ffn_state["ln2_pvf"] = prev_varf
            ln2_v = prev_vout

        for tb in range(2):
            tg0 = tb * TB
            uevac_v = {}
            uevac_s = {}
            vt_cons = {}
            vt = 0
            for ot in range(32):
                key, sl, gcnt = gdma(lambda h, sl, ot=ot: h.dma_start(out=PB[sl][:, 0:2048], in_=w12[ot]))
                P.wait("t", "g%d" % sl, gcnt)
                if tb == 0 and ot == 0:
                    P.wait("t", "s", ln1_cast_s[1])
                    P.wait("t", "v", opv_last[NW - 1])
                    P.wait("t", "v", ret_v_done[NW - 1])
                    P.wait("t", "s", ret_s_done[NW - 1])
                if tb == 1 and ot == 0:
                    P.wait("t", "v", prev_out_v)
                if tb == 1 and ot == 2:
                    do_ln2(0, ths=(0, 1))
                for wi in range(2):
                    pair = vt % 3
                    if tb == 0 and pair == 1 and vt < 3:
                        P.wait("t", "v", ln1_varf[NW - 1])
                    if vt >= 3:
                        eng, cval = vt_cons[vt - 3]
                        P.wait("t", eng, cval)
                    for c in range(8):
                        for th in range(2):
                            P.emit("t", lambda h, wi=wi, th=th, c=c, pair=pair, sl=sl, tg0=tg0:
                                   h.matmul(PS[pair * 2 + th][:],
                                            PB[sl][:, wi * 1024 + c * 128: wi * 1024 + c * 128 + 128],
                                            x1t(c, tg0 + th * 512, 512),
                                            start=(c == 0), stop=(c == 7)), 1)
                    t_u = P.cnt["t"]
                    if wi == 0:
                        P.wait("s", "t", t_u)
                        if ot >= 1:
                            P.wait("s", "v", uevac_v[ot - 1])
                        if tb == 0 and ot == 0:
                            P.wait("s", "v", ln1_sub_v[NW - 1])
                        for th in range(2):
                            P.emit("s", lambda h, ot=ot, th=th, pair=pair:
                                   h.activation(tmpa(th), PS[pair * 2 + th][:], AF.Silu,
                                                scale=S1C[:, ot:ot + 1]), 1)
                        uevac_s[ot] = P.cnt["s"]
                        vt_cons[vt] = ("s", uevac_s[ot])
                    else:
                        P.wait("v", "s", uevac_s[ot])
                        P.wait("v", "t", t_u)
                        if tb == 0 and ot == 8:
                            P.wait("v", "s", ln1_cast_s[NW - 1])
                        for th in range(2):
                            P.emit("v", lambda h, ot=ot, th=th, pair=pair:
                                   h.scalar_tensor_tensor(ub(ot, th * 512, 512), PS[pair * 2 + th][:],
                                                          S2C[:, ot:ot + 1], tmpa(th),
                                                          ALU.mult, ALU.mult), 1)
                        uevac_v[ot] = P.cnt["v"]
                        vt_cons[vt] = ("v", uevac_v[ot])
                    vt += 1
                pb_free_t[key] = P.cnt["t"]
            if debug and tb == 0:
                P.wait("q", "v", P.cnt["v"])
                P.emit("q", lambda h: h.dma_start(out=taps["tapU"], in_=ABa[:]), 16, sem="qi")
                P.wait("s", "qi", P.cnt["qi"])
                P.wait("v", "qi", P.cnt["qi"])
                P.wait("t", "qi", P.cnt["qi"])

            h1evac = {}
            for ot in range(32):
                key, sl, gcnt = gdma(lambda h, sl, ot=ot: h.dma_start(out=PB[sl][:], in_=h1p[ot]))
                P.wait("t", "g%d" % sl, gcnt)
                if ot == 0:
                    banks = (PKF[0][:, 0:512], PKF[1][:, 0:512])
                    P.wait("t", "v", ffn_state.get("ln2_pvf", 0))
                elif ot == 1:
                    banks = (PS[2][:], PS[3][:])
                    P.wait("t", "v", uevac_v[30])
                elif ot == 2:
                    banks = (PS[4][:], PS[5][:])
                    P.wait("t", "s", uevac_s[31])
                else:
                    banks = (PS[(ot % 3) * 2][:], PS[(ot % 3) * 2 + 1][:])
                    P.wait("t", "s", h1evac[ot - 3])
                for c in range(32):
                    P.wait("t", "v", uevac_v[c])
                    for th in range(2):
                        P.emit("t", lambda h, c=c, th=th, bk=banks[th], sl=sl:
                               h.matmul(bk, PB[sl][:, c * 128: c * 128 + 128],
                                        ub(c, th * 512, 512), start=(c == 0), stop=(c == 31)), 1)
                pb_free_t[key] = P.cnt["t"]
                t_h = P.cnt["t"]
                P.wait("s", "t", t_h)
                P.wait("s", "qs", store_q)
                for th in range(2):
                    P.emit("s", lambda h, ot=ot, th=th, bk=banks[th]:
                           h.activation(h1b(ot, th * 512, 512), bk, AF.Gelu,
                                        scale=S1HC[:, ot:ot + 1]), 1)
                h1evac[ot] = P.cnt["s"]
            if debug and tb == 0:
                P.wait("q", "s", P.cnt["s"])
                P.emit("q", lambda h: h.dma_start(out=taps["tapH1"], in_=AAa[:]), 16, sem="qi")
                P.wait("s", "qi", P.cnt["qi"])
                P.wait("v", "qi", P.cnt["qi"])
                P.wait("t", "qi", P.cnt["qi"])

            h2evac = {}
            for ot in range(32):
                key, sl, gcnt = gdma(lambda h, sl, ot=ot: h.dma_start(out=PB[sl][:], in_=h2p[ot]))
                P.wait("t", "g%d" % sl, gcnt)
                if ot == 0:
                    banks = (PKF[0][:, 0:512], PKF[1][:, 0:512])
                    P.wait("t", "s", h1evac[0])
                elif ot == 1:
                    banks = (PS[2][:], PS[3][:])
                    P.wait("t", "s", h1evac[31])
                elif ot == 2:
                    banks = (PS[4][:], PS[5][:])
                    P.wait("t", "s", h1evac[29])
                else:
                    banks = (PS[(ot % 3) * 2][:], PS[(ot % 3) * 2 + 1][:])
                    P.wait("t", "s", h2evac[ot - 3])
                    if ot == 3:
                        P.wait("t", "s", h1evac[30])
                for c in range(32):
                    P.wait("t", "s", h1evac[c])
                    for th in range(2):
                        P.emit("t", lambda h, c=c, th=th, bk=banks[th], sl=sl:
                               h.matmul(bk, PB[sl][:, c * 128: c * 128 + 128],
                                        h1b(c, th * 512, 512), start=(c == 0), stop=(c == 31)), 1)
                pb_free_t[key] = P.cnt["t"]
                t_h = P.cnt["t"]
                P.wait("s", "t", t_h)
                for th in range(2):
                    P.emit("s", lambda h, ot=ot, th=th, bk=banks[th]:
                           h.activation(h2b(ot, th * 512, 512), bk, AF.Gelu,
                                        scale=S2HC[:, ot:ot + 1]), 1)
                h2evac[ot] = P.cnt["s"]

            outevac = {}
            for ot in range(8):
                key, sl, gcnt = gdma(lambda h, sl, ot=ot: h.dma_start(out=PB[sl][:], in_=w3p[ot]))
                P.wait("t", "g%d" % sl, gcnt)
                if ot == 0:
                    banks = (PKF[0][:, 0:512], PKF[1][:, 0:512])
                    P.wait("t", "s", h2evac[0])
                elif ot == 1:
                    banks = (PS[2][:], PS[3][:])
                    P.wait("t", "s", h2evac[31])
                elif ot == 2:
                    banks = (PS[4][:], PS[5][:])
                    P.wait("t", "s", h2evac[29])
                else:
                    banks = (PS[(ot % 3) * 2][:], PS[(ot % 3) * 2 + 1][:])
                    P.wait("t", "v", outevac[ot - 3])
                    if ot == 3:
                        P.wait("t", "s", h2evac[30])
                for c in range(32):
                    P.wait("t", "s", h2evac[c])
                    for th in range(2):
                        P.emit("t", lambda h, c=c, th=th, bk=banks[th], sl=sl:
                               h.matmul(bk, PB[sl][:, c * 128: c * 128 + 128],
                                        h2b(c, th * 512, 512), start=(c == 0), stop=(c == 31)), 1)
                pb_free_t[key] = P.cnt["t"]
                t_o = P.cnt["t"]
                P.wait("v", "t", t_o)
                if ot == 0:
                    P.wait("v", "s", ln1_cast_s[NW - 1])
                for th in range(2):
                    P.emit("v", lambda h, ot=ot, th=th, bk=banks[th], tg0=tg0:
                           h.scalar_tensor_tensor(r2b(ot, th * 512, 512), bk,
                                                  S3C[:, ot:ot + 1], x1t(ot, tg0 + th * 512, 512),
                                                  ALU.mult, ALU.add), 1)
                outevac[ot] = P.cnt["v"]
            out_v = P.cnt["v"]
            prev_out_v = out_v
            ffn_state["out_v"][tb] = out_v

            if tb == 1:
                do_ln2(1, ths=(0, 1))

        P.wait_self_dma("q", "qs", P.cnt["qs"])
        P.emit("q", lambda h: h.nop())

        block = e(nc.Block())

        @block.tensor
        def _(t):
            P.run("t", t, sems)

        @block.vector
        def _(v):
            P.run("v", v, sems)

        @block.scalar
        def _(s):
            P.run("s", s, sems)

        @block.gpsimd
        def _(g):
            P.run("g", g, sems)

        @block.sync
        def _(q):
            P.run("q", q, sems)

    return nc


def _panels(w):
    O, I = w.shape
    no, ni = O // 128, I // 128
    return np.ascontiguousarray(w.reshape(no, 128, ni, 128).transpose(0, 3, 2, 1)).reshape(no, 128, I)


def _colvec(v):
    return np.ascontiguousarray(v.reshape(-1, 128).T)


def _group4(panels):
    n = panels.shape[0]
    return np.ascontiguousarray(panels.reshape(n // 4, 4, 128, 1024).transpose(0, 2, 1, 3)).reshape(n // 4, 128, 4096)


def kernel(**inputs):
    import ml_dtypes

    x = inputs["L0_attn_x"]
    wqkv = inputs["L0_attn_Wqkv_w"]
    bqkv = inputs["L0_attn_Wqkv_b"]

    qk_panels = _panels(wqkv[0:2048])
    wv = wqkv[2048:3072]
    wvr_c = np.ascontiguousarray(wv.reshape(1024, 8, 128).transpose(1, 2, 0))
    fin = dict(
        wqk=_group4(qk_panels),
        wvr=_group4(wvr_c),
        vb=np.ascontiguousarray(bqkv[2048:3072]).reshape(1, 1024),
        qkb=_colvec(bqkv[0:2048]),
        outw=_group4(_panels(inputs["L0_attn_out_w"])),
        outb=_colvec(inputs["L0_attn_out_b"]),
        w12=np.ascontiguousarray(
            np.concatenate([_panels(inputs["L0_ffn_w1_w"]), _panels(inputs["L0_ffn_w2_w"])], axis=2)),
        s1c=_colvec(inputs["L0_ffn_w1_s"]), s2c=_colvec(inputs["L0_ffn_w2_s"]),
        h1p=_panels(inputs["L0_ffn_h1_w"]), s1hc=_colvec(inputs["L0_ffn_h1_s"]),
        h2p=_panels(inputs["L0_ffn_h2_w"]), s2hc=_colvec(inputs["L0_ffn_h2_s"]),
        w3p=_panels(inputs["L0_ffn_w3_w"]), s3c=_colvec(inputs["L0_ffn_w3_s"]),
        gam=np.ascontiguousarray(np.broadcast_to(inputs["L0_attn_gamma_raw"].reshape(1, H), (128, H))),
        ident=np.eye(128, dtype=ml_dtypes.bfloat16),
        emat=np.maximum(np.arange(128)[None, :] - np.arange(128)[:, None], 0).astype(np.float32),
        tri=(0.125 * (np.arange(128)[:, None] <= np.arange(128)[None, :])).astype(np.float32),
        t1m=np.broadcast_to((np.arange(128) + 1).astype(np.float32), (128, 128)).copy(),
        l0125=np.full((128, 1), np.log(0.125), np.float32),
        c128=np.full((128, 1), 128.0, np.float32),
        onesr=np.ones((1, 128), np.float32),
        onesbf=np.ones((1, 128), ml_dtypes.bfloat16),
        onescolb=np.ones((128, 1), ml_dtypes.bfloat16),
        epsc=np.full((128, 1), 1e-5, np.float32),
        rcpd=np.full((1, 1), 1.0 / D, np.float32),
        negh=np.full((128, 1), -0.5, np.float32),
        cvr=np.broadcast_to((127.0 - np.arange(128)).astype(np.float32).reshape(128, 1), (128, 64)).copy(),
    )

    nc = build(debug=False)
    in_maps = []
    for b in range(NCORE):
        m = dict(fin)
        m["xT"] = np.ascontiguousarray(x[b].T)
        in_maps.append(m)
    res = run_bass_kernel_spmd(nc, in_maps, core_ids=list(range(NCORE)))
    out = np.stack([np.ascontiguousarray(res.results[b]["yT"].T) for b in range(NCORE)], axis=0)
    return out
